# version 01
# baseline (speedup 1.0000x reference)
import math
import os
import numpy as np
import ml_dtypes
from contextlib import ExitStack

import concourse.bass as bass
import concourse.mybir as mybir
from concourse.bass_utils import run_bass_kernel_spmd
from concourse.bass_core_rust import IndirectOffsetOnAxis

DBG_DRAM = set(os.environ.get("MK_DUMP_DRAM", "").split(",")) - {""}

F32 = mybir.dt.float32
BF16 = mybir.dt.bfloat16
I32 = mybir.dt.int32
AF = mybir.ActivationFunctionType
ALU = mybir.AluOpType
AX = mybir.AxisListType


ENGS = ("pe", "act", "dve", "pool", "sp")


class Buf:
    __slots__ = ("name", "writer", "readers")

    def __init__(self, name):
        self.name = name
        self.writer = None
        self.readers = []


class Op:
    __slots__ = ("idx", "eng", "emit", "deps", "kind", "signaled", "sig")

    def __init__(self, idx, eng, emit, deps, kind):
        self.idx = idx
        self.eng = eng
        self.emit = emit
        self.deps = deps
        self.kind = kind
        self.signaled = False
        self.sig = None


class Prog:

    def __init__(self, nc):
        self.nc = nc
        self.ops = []
        self.bufs = {}

    def buf(self, name):
        b = self.bufs.get(name)
        if b is None:
            b = self.bufs[name] = Buf(name)
        return b

    def op(self, eng, emit, reads=(), writes=(), kind="c"):
        i = len(self.ops)
        deps = set()
        for b in reads:
            if b.writer is not None:
                deps.add(b.writer)
        for b in writes:
            if b.writer is not None:
                deps.add(b.writer)
            deps.update(b.readers)
        deps.discard(i)
        o = Op(i, eng, emit, deps, kind)
        self.ops.append(o)
        for b in reads:
            b.readers.append(i)
        for b in writes:
            b.writer = i
            b.readers = []
        return o

    def dma(self, eng, dst_ap, src_ap, reads=(), writes=()):
        return self.op(eng, lambda e: e.dma_start(dst_ap, src_ap),
                       reads=reads, writes=writes, kind="d")

    def finalize_and_emit(self):
        nc = self.nc
        ops = self.ops
        for o in ops:
            kept = set()
            for d in o.deps:
                p = ops[d]
                if p.kind == "c" and p.eng == "pe" and o.eng == "pe":
                    continue
                kept.add(d)
                if p.kind == "c":
                    p.signaled = True
            o.deps = kept
        sems = {}
        with ExitStack() as st:
            for e in ENGS:
                sems[("c", e)] = st.enter_context(nc.semaphore(f"sc_{e}"))
                sems[("d", e)] = st.enter_context(nc.semaphore(f"sd_{e}"))
            sems[("coll", "pool")] = st.enter_context(nc.semaphore("s_coll"))
            counts = {k: 0 for k in sems}
            for o in ops:
                if o.kind == "d":
                    counts[("d", o.eng)] += 16
                    o.sig = (("d", o.eng), counts[("d", o.eng)])
                elif o.kind == "coll":
                    counts[("coll", "pool")] += 1
                    o.sig = (("coll", "pool"), counts[("coll", "pool")])
                elif o.signaled:
                    counts[("c", o.eng)] += 1
                    o.sig = (("c", o.eng), counts[("c", o.eng)])
            final_counts = dict(counts)

            per_eng = {e: [o for o in ops if o.eng == e] for e in ENGS}
            nc_eng = {"pe": nc.tensor, "act": nc.scalar, "dve": nc.vector,
                      "pool": nc.gpsimd, "sp": nc.sync}

            serial_dma = os.environ.get("MK_SERIAL_DMA", "1") != "0"

            def run_engine(e, engine):
                vclock = {}
                ndma = 0
                for o in per_eng[e]:
                    if o.kind == "d" and serial_dma and ndma > 0:
                        key = ("d", e)
                        if vclock.get(key, 0) < 16 * ndma:
                            engine.wait_ge(sems[key], 16 * ndma)
                            vclock[key] = 16 * ndma
                    if o.kind == "d":
                        ndma += 1
                    for d in sorted(o.deps):
                        p = ops[d]
                        if p.sig is None:
                            continue
                        key, val = p.sig
                        if vclock.get(key, 0) >= val:
                            continue
                        engine.wait_ge(sems[key], val)
                        vclock[key] = val
                    inst = o.emit(engine)
                    if o.sig is not None:
                        key, _ = o.sig
                        n = 16 if key[0] == "d" else 1
                        inst.then_inc(sems[key], n)
                if e == "sp":
                    for key, val in final_counts.items():
                        if val > 0 and vclock.get(key, 0) < val:
                            engine.wait_ge(sems[key], val)

            with nc.Block() as block:
                block.tensor(lambda eng: run_engine("pe", eng))
                block.scalar(lambda eng: run_engine("act", eng))
                block.vector(lambda eng: run_engine("dve", eng))
                block.gpsimd(lambda eng: run_engine("pool", eng))
                block.sync(lambda eng: run_engine("sp", eng))



def _bucket_ref(d):
    NUM_BUCKETS, MAX_DIST = 32, 128
    rp = max(d, 0)
    max_exact = NUM_BUCKETS // 2
    if rp < max_exact:
        return rp
    v = np.float32(np.log(np.float32(rp) / np.float32(max_exact)))
    v = np.float32(v / np.float32(math.log(MAX_DIST / max_exact)))
    v = np.float32(v * np.float32(NUM_BUCKETS - max_exact))
    b = max_exact + int(v)
    return min(b, NUM_BUCKETS - 1)


def _host_constants(j):
    C = {}
    ohb = np.zeros((32, 1152), np.float32)
    for i in range(1152):
        ohb[_bucket_ref(i - 512), i] = 1.0
    C["OHB"] = ohb
    p = np.arange(128, dtype=np.int32)
    cols = []
    for m in (-1, 0, 1, 2, 3):
        delta = -128 * m
        cols.append(delta + 512 - p)
    for lp in range(4):
        cols.append((j * 512 + lp * 128 + p) * 1024)
    for g in range(2):
        cols.append(((2 * j + g) * 1024 + p) * 260 if False else ((2 * j + g) * 1024 + p) * 256)
    cols.append((j * 128 + p) * 1024)
    for ql in range(4):
        cols.append(((4 * j + ql) * 128 + p) * 1024)
    cols.append(((j // 2) * 1024 + p) * 256 + (j % 2) * 128)
    cols.append(p * 4096 + j * 1024)
    idx = np.stack(cols, axis=1).astype(np.int32)
    C["IDXALL"] = np.ascontiguousarray(idx)
    qf = np.arange(512)
    kk = np.arange(128)
    msk = np.zeros((8, 128, 512), np.float32)
    for mi, m in enumerate(range(-4, 4)):
        d = (-128 * m) + qf[None, :] - kk[:, None]
        msk[mi] = ((d >= 0) & (d <= 511)).astype(np.float32)
    C["MSK1"] = msk.astype(ml_dtypes.bfloat16)
    C["CONSTS"] = np.tile(np.array(
        [[1e-5, -40.0, 1.0, 0.0, 1.0 / 2048.0, 1e-6, 1e-20, 0.0]], np.float32),
        (128, 1))
    C["ONESBF"] = np.ones((128, 128), ml_dtypes.bfloat16)
    C["ONESF"] = np.ones((128, 128), np.float32)
    C["IDENF"] = np.eye(128, dtype=np.float32)
    C["IDENBF"] = np.eye(128, dtype=ml_dtypes.bfloat16)
    oh8 = np.zeros((8, 8 * 128), np.float32)
    for e in range(8):
        oh8[e, e * 128:(e + 1) * 128] = 1.0
    C["OH8"] = oh8.astype(ml_dtypes.bfloat16)
    oh16 = np.zeros((16, 16 * 128), np.float32)
    for e in range(16):
        oh16[e, e * 128:(e + 1) * 128] = 1.0
    C["OH16"] = oh16.astype(ml_dtypes.bfloat16)
    return C


def _rope_tables(S=4096, HEAD_DIM=128, THETA=10000.0):
    inv_freq = 1.0 / THETA ** (np.arange(0, HEAD_DIM, 2, dtype=np.float32) / HEAD_DIM)
    freqs = np.arange(S, dtype=np.float32)[:, None] * inv_freq[None, :]
    return np.cos(freqs), np.sin(freqs)


BF = ml_dtypes.bfloat16


class View:
    __slots__ = ("ap", "bufs")

    def __init__(self, ap, bufs):
        self.ap = ap
        self.bufs = bufs


class MK:
    def __init__(self, nc, prog):
        self.nc = nc
        self.P = prog
        self.d = {}
        self.s = {}
        self.v = {}
        self._arena_ranges = {}
        self._pending = []

    def dram(self, name, shape, dtype, kind="ExternalInput"):
        ap = self.nc.dram_tensor(name, shape, dtype, kind=kind).ap()
        self.d[name] = ap
        return ap

    def sbuf(self, st, name, shape, dtype):
        t = st.enter_context(self.nc.sbuf_tensor(name, shape, dtype))
        self.s[name] = t
        v = View(t[:], [self.P.buf("sb:" + name)])
        self.v[name] = v
        return v

    def arena_view(self, arena, name, off_bytes, dtype, cols):
        esz = 4 if dtype in (F32, I32) else 2
        t = self.s[arena]
        base = t[:].bitcast(dtype) if dtype != F32 else t[:]
        ap = base[:, off_bytes // esz: off_bytes // esz + cols]
        nbytes = cols * esz
        self._arena_ranges.setdefault(arena, []).append(
            (off_bytes, off_bytes + nbytes, name))
        v = View(ap, None)
        self.v[name] = v
        self._pending.append((arena, name, off_bytes, off_bytes + nbytes, v))
        return v

    def finish_views(self):
        for arena, name, s0, e0, v in self._pending:
            bufs = [self.P.buf(f"ar:{arena}:{name}")]
            for (s1, e1, other) in self._arena_ranges[arena]:
                if other != name and s1 < e0 and s0 < e1:
                    bufs.append(self.P.buf(f"ar:{arena}:{other}"))
            v.bufs = bufs
        self._pending = []


def build_program():
    nc = bass.Bass("TRN2", target_bir_lowering=False, debug=False)
    prog = Prog(nc)
    mk = MK(nc, prog)
    d = mk.dram

    d("X0", [2048, 1024], F32)
    d("W0Q", [2048, 2048], F32); d("W0K", [2048, 2048], F32)
    d("W0V", [2048, 2048], F32); d("W0O", [2048, 2048], F32)
    d("W0G", [2048, 8], F32)
    d("W01T", [8, 2048, 1024], F32); d("W02T", [8, 1024, 2048], F32)
    d("RELB", [32, 8], F32); d("OHB", [32, 1152], F32)
    d("W1Q", [2048, 2048], F32); d("W1K", [2048, 512], F32)
    d("W1V", [2048, 512], F32); d("W1O", [2048, 2048], F32)
    d("LAMC", [1, 4], F32); d("G1T", [2048, 16], F32)
    d("W13T", [16, 2048, 2816], F32); d("W12T", [16, 1408, 2048], F32)
    d("SW13T", [2048, 5632], F32); d("SW2T", [2816, 2048], F32)
    d("COS2", [128, 1024], BF16); d("SIN2", [128, 1024], BF16)
    d("MSK1D", [8, 128, 512], BF16)
    d("CONSTS", [128, 8], F32)
    d("ONESBF", [128, 128], BF16); d("ONESF", [128, 128], F32)
    d("IDENF", [128, 128], F32); d("IDENBF", [128, 128], BF16)
    d("OH8", [8, 1024], BF16); d("OH16", [16, 2048], BF16)
    d("IDXALL", [128, 18], I32)
    d("OUT", [2048, 1024], F32, kind="ExternalOutput")
    ik = lambda n: "Internal"
    d("X1", [2048, 1024], F32, kind=ik("X1"))
    d("BROWD", [8, 1152], BF16, kind=ik("BROWD"))
    d("C0I", [24576, 256], BF16, kind=ik("C0I"))
    d("C0O", [4 * 24576, 256], BF16, kind=ik("C0O"))
    d("C1I", [12288, 256], BF16, kind=ik("C1I"))
    d("C1O", [4 * 12288, 256], BF16, kind=ik("C1O"))
    d("OC0I", [8192, 256], BF16, kind=ik("OC0I"))
    d("OC0O", [32768, 256], BF16, kind=ik("OC0O"))
    d("OC1I", [8192, 256], BF16, kind=ik("OC1I"))
    d("OC1O", [32768, 256], BF16, kind=ik("OC1O"))

    st = ExitStack()
    mk.st = st
    sb = lambda name, shape, dt: mk.sbuf(st, name, shape, dt)

    sb("XN", [128, 16384], BF16)
    sb("GS", [128, 16384], BF16)
    sb("AR", [128, 16384], F32)
    sb("A2", [128, 8192], F32)
    sb("WSa", [128, 2048], BF16); sb("WSb", [128, 2048], BF16)
    sb("W2a", [128, 512], BF16); sb("W2b", [128, 512], BF16)
    sb("KSTa", [128, 1024], BF16); sb("KSTb", [128, 1024], BF16)
    sb("RSH", [128, 1024], BF16)
    sb("CS2", [128, 1024], BF16); sb("SN2", [128, 1024], BF16)
    sb("BSH", [128, 256], F32)
    sb("ROWS", [128, 1024], F32)
    sb("CONSTS_S", [128, 8], F32)
    sb("ONESBF_S", [128, 128], BF16); sb("ONESF_S", [128, 128], F32)
    sb("IDENF_S", [128, 128], F32); sb("IDENBF_S", [128, 128], BF16)
    sb("OH8_S", [8, 1024], BF16); sb("OH16_S", [16, 2048], BF16)
    sb("IDXS", [128, 18], I32)
    sb("LAM_S", [1, 4], F32)
    sb("SCR", [128, 2048], F32)
    sb("CBF", [16, 1024], BF16)
    sb("CB", [128, 1024], F32)

    av = mk.arena_view
    for dc in range(16):
        av("AR", f"x{dc}", dc * 4096, F32, 1024)
    for dc in range(16):
        av("AR", f"y{dc}", dc * 4096, F32, 1024)
    av("AR", "KP", 0, BF16, 4096)
    av("AR", "VA", 8192, BF16, 8320)
    av("AR", "QL", 24832, BF16, 4096)
    av("AR", "VG", 47360, BF16, 8192)
    av("AR", "EA", 33024, BF16, 1024)
    av("AR", "EB2", 35072, BF16, 1024)
    for h in range(2):
        for m in range(5):
            av("AR", f"BND{h}_{m}", 37120 + (h * 5 + m) * 1024, BF16, 512)
    av("AR", "K1", 0, BF16, 4096)
    av("AR", "V1", 8192, BF16, 4096)
    av("AR", "Q1", 16384, BF16, 4096)
    av("AR", "E1B", 24576, BF16, 8192)
    av("AR", "D1B", 40960, BF16, 4096)
    av("AR", "BR4", 49152, BF16, 2048)
    av("AR", "MSKL", 53248, BF16, 4096)
    av("A2", "SQ", 0, F32, 1024)
    av("A2", "HH", 4096, BF16, 1024); av("A2", "HL", 6144, BF16, 1024)
    av("A2", "H2H", 8192, BF16, 1024); av("A2", "H2L", 10240, BF16, 1024)
    for hb in range(8):
        av("A2", f"h{hb}", hb * 2048, BF16, 1024)
    for gi in range(11):
        av("A2", f"a{gi}", gi * 2048, BF16, 1024)
    av("A2", "UP", 22528, BF16, 1024)
    av("A2", "GTA", 16384, F32, 1536)
    av("A2", "BROWV", 28672, BF16, 1152)
    av("A2", "PK", 24576, BF16, 1024)
    av("A2", "PK2", 26624, BF16, 1024)
    mk.finish_views()

    b = prog.buf
    return nc, prog, mk, st


def emit_layers(mk, P):
    nc, prog = mk.nc, mk.P
    V = mk.v
    D = mk.d
    b = prog.buf

    Pt = P
    pb = [[b(f"ps{i}a"), b(f"ps{i}b")] for i in range(4)]

    def pview(i, half=None, rows=None, cols=None):
        ap = Pt[i][:]
        if half is not None:
            ap = Pt[i][:, half * 512:(half + 1) * 512]
        if cols is not None:
            ap = Pt[i][:, cols[0]:cols[1]]
        if rows is not None:
            lo, hi = rows
            ap = ap[lo:hi]
        return ap

    def pbufs(i, half=None):
        if half is None:
            return pb[i]
        return [pb[i][half]]

    dbuf = {}
    def drbuf(name):
        return b("dr:" + name)

    def mm(out_ap, lhsT, rhs, wb, rb, start=True, stop=True, tp=None):
        prog.op("pe", lambda e, o=out_ap, l=lhsT, r=rhs, s=start, t=stop, q=tp:
                e.matmul(o, l, r, start=s, stop=t, tile_position=q,
                         skip_group_check=True),
                reads=rb, writes=wb)

    def pe_tr(out_ap, in_ap, iden, wb, rb):
        prog.op("pe", lambda e, o=out_ap, i=in_ap, d=iden: e.transpose(o, i, d),
                reads=rb, writes=wb)

    def act(eng_out, in_ap, func, wb, rb, bias=0.0, scale=1.0, accum=None):
        prog.op("act", lambda e, o=eng_out, i=in_ap, f=func, bi=bias, sc=scale,
                a=accum: e.activation(o, i, f, bias=bi, scale=sc, accum_out=a),
                reads=rb, writes=wb)

    def dve(fn, wb, rb, eng="dve"):
        prog.op(eng, fn, reads=rb, writes=wb)

    def dma(eng, dst, src, wb, rb):
        prog.dma(eng, dst, src, reads=rb, writes=wb)

    def idma(dst, in_ap, idxcol, eoff, wb, rb):
        idx = V["IDXS"].ap[:, idxcol:idxcol + 1]
        prog.op("pool", lambda e, d_=dst, i_=in_ap, x_=idx, o_=eoff:
                e.indirect_dma_start(d_, None, i_, IndirectOffsetOnAxis(x_, 0),
                                     element_offset=o_),
                reads=rb + [b("sb:IDXS")], writes=wb, kind="d")

    def coll(kind, ins, outs, wb, rb):
        groups = [[0, 1, 2, 3], [4, 5, 6, 7]]
        prog.op("pool", lambda e, k=kind, i=ins, o=outs, g=groups:
                e.collective_compute(k, ALU.bypass, replica_groups=g,
                                     ins=[i.opt()], outs=[o.opt()]),
                reads=rb, writes=wb, kind="coll")

    def flat_ap(name, n):
        return bass.AP(D[name].tensor, 0, [[1, n], [1, 1]])

    ONESBF = V["ONESBF_S"].ap
    ONESF = V["ONESF_S"].ap
    CONSTS = V["CONSTS_S"].ap
    XN = mk.s["XN"][:]
    GS = mk.s["GS"][:]
    xnb = V["XN"].bufs
    gsb = V["GS"].bufs

    def xn_t(dc, nt=None):
        if nt is None:
            return XN.bitcast(BF16)[:, dc * 1024:(dc + 1) * 1024] if False else \
                mk.s["XN"][:, dc * 1024:(dc + 1) * 1024]
        return mk.s["XN"][:, dc * 1024 + nt * 512: dc * 1024 + (nt + 1) * 512]

    for nm, dst in (("CONSTS", "CONSTS_S"), ("ONESBF", "ONESBF_S"),
                    ("ONESF", "ONESF_S"), ("IDENF", "IDENF_S"),
                    ("IDENBF", "IDENBF_S"), ("OH8", "OH8_S"),
                    ("OH16", "OH16_S"), ("IDXALL", "IDXS"),
                    ("LAMC", "LAM_S"), ("COS2", "CS2"), ("SIN2", "SN2")):
        dma("sp", V[dst].ap, D[nm], V[dst].bufs, [])
    dma("sp", V["MSKL"].ap.rearrange("p (m f) -> p m f", f=512),
        D["MSK1D"].rearrange("m p f -> p m f"), V["MSKL"].bufs, [])

    dma("pool", mk.s["W2a"][0:32, 0:8], D["RELB"], V["W2a"].bufs, [])
    dma("pool", mk.s["WSa"][0:32, 0:1152], D["OHB"], V["WSa"].bufs, [])
    mm(pview(0, cols=(0, 512), rows=(0, 8)), mk.s["W2a"][0:32, 0:8],
       mk.s["WSa"][0:32, 0:512], pbufs(0, 0), V["W2a"].bufs + V["WSa"].bufs)
    mm(pview(0, cols=(512, 1024), rows=(0, 8)), mk.s["W2a"][0:32, 0:8],
       mk.s["WSa"][0:32, 512:1024], pbufs(0, 1), V["W2a"].bufs + V["WSa"].bufs)
    mm(pview(1, cols=(0, 128), rows=(0, 8)), mk.s["W2a"][0:32, 0:8],
       mk.s["WSa"][0:32, 1024:1152], pbufs(1, 0), V["W2a"].bufs + V["WSa"].bufs)
    BROWV = V["BROWV"]
    dve(lambda e: e.tensor_copy(BROWV.ap[0:8, 0:1024], pview(0, rows=(0, 8))),
        BROWV.bufs, pbufs(0))
    dve(lambda e: e.tensor_copy(BROWV.ap[0:8, 1024:1152],
                                pview(1, cols=(0, 128), rows=(0, 8))),
        BROWV.bufs, pbufs(1))
    dma("sp", D["BROWD"], BROWV.ap[0:8, :], [drbuf("BROWD")], BROWV.bufs)
    dma("sp", mk.s["SCR"][0:1, 0:256],
        bass.AP(D["RELB"].tensor, 0, [[256, 1], [1, 256]]), V["SCR"].bufs, [])
    mm(pview(2, cols=(0, 256)), ONESF[0:1, 0:128], mk.s["SCR"][0:1, 0:256],
       pbufs(2, 0), V["SCR"].bufs + V["ONESF_S"].bufs)
    dve(lambda e: e.tensor_scalar(V["BSH"].ap, pview(2, cols=(0, 256)), -40.0,
                                  None, ALU.add),
        V["BSH"].bufs, pbufs(2, 0))

    def layer(L):
        xsrc = D["X0"] if L == 0 else D["X1"]
        xsrc_buf = [drbuf("X1")] if L == 1 else []
        xdst = D["X1"] if L == 0 else D["OUT"]
        xdst_buf = drbuf("X1") if L == 0 else drbuf("OUT")
        CIN = D["C0I"] if L == 0 else D["C1I"]
        COUT = D["C0O"] if L == 0 else D["C1O"]
        OCI = D["OC0I"] if L == 0 else D["OC1I"]
        OCO = D["OC0O"] if L == 0 else D["OC1O"]
        cinb, coutb = drbuf("CIN%d" % L), drbuf("COUT%d" % L)
        ocib, ocob = drbuf("OCI%d" % L), drbuf("OCO%d" % L)
        rows_cin = 24576 if L == 0 else 12288
        vbase = 8192 * 256 if L == 0 else 2048 * 256
        qbase = 16384 * 256 if L == 0 else 4096 * 256

        xv = [V[f"x{dc}"] for dc in range(16)]
        for dc in range(16):
            dma("sp", xv[dc].ap,
                bass.AP(xsrc.tensor, dc * 128 * 1024, [[1024, 128], [1, 1024]]),
                xv[dc].bufs, xsrc_buf)
        SQ, HH, HL, H2H, H2L = V["SQ"], V["HH"], V["HL"], V["H2H"], V["H2L"]
        for dc in range(16):
            act(SQ.ap, xv[dc].ap, AF.Square, SQ.bufs, xv[dc].bufs)
            dve(lambda e, d=dc: e.tensor_copy(HH.ap, xv[d].ap), HH.bufs, xv[dc].bufs)
            dve(lambda e, d=dc: e.tensor_tensor(HL.ap, xv[d].ap, HH.ap, ALU.subtract),
                HL.bufs, xv[dc].bufs + HH.bufs)
            dve(lambda e: e.tensor_copy(H2H.ap, SQ.ap), H2H.bufs, SQ.bufs)
            dve(lambda e: e.tensor_tensor(H2L.ap, SQ.ap, H2H.ap, ALU.subtract),
                H2L.bufs, SQ.bufs + H2H.bufs)
            for nt in range(2):
                for si, src in enumerate((HH, HL)):
                    mm(pview(2, half=nt, rows=(0, 1)), ONESBF[:, 0:1],
                       src.ap[:, nt * 512:(nt + 1) * 512], pbufs(2, nt),
                       src.bufs + V["ONESBF_S"].bufs,
                       start=(dc == 0 and si == 0), stop=(dc == 15 and si == 1))
                for si, src in enumerate((H2H, H2L)):
                    mm(pview(3, half=nt, rows=(0, 1)), ONESBF[:, 0:1],
                       src.ap[:, nt * 512:(nt + 1) * 512], pbufs(3, nt),
                       src.bufs + V["ONESBF_S"].bufs,
                       start=(dc == 0 and si == 0), stop=(dc == 15 and si == 1))
        SCR = mk.s["SCR"]
        scrb = V["SCR"].bufs
        ROWS = mk.s["ROWS"]
        rwb = V["ROWS"].bufs
        dve(lambda e: e.tensor_scalar(SCR[0:1, 0:1024], Pt[2][0:1, :],
                                      1.0 / 2048.0, None, ALU.mult), scrb, pbufs(2))
        dve(lambda e: e.tensor_scalar(SCR[0:1, 1024:2048], Pt[3][0:1, :],
                                      1.0 / 2048.0, None, ALU.mult), scrb, pbufs(3))
        dve(lambda e: e.tensor_tensor(ROWS[0:1, 0:1024], SCR[0:1, 0:1024],
                                      SCR[0:1, 0:1024], ALU.mult), rwb, scrb)
        dve(lambda e: e.tensor_tensor(SCR[0:1, 1024:2048], SCR[0:1, 1024:2048],
                                      ROWS[0:1, 0:1024], ALU.subtract), scrb,
            scrb + rwb)
        act(SCR[0:1, 1024:2048], SCR[0:1, 1024:2048], AF.Ln, scrb, scrb,
            bias=CONSTS[0:1, 0:1])
        act(SCR[0:1, 1024:2048], SCR[0:1, 1024:2048], AF.Exp, scrb, scrb,
            scale=-0.5)
        dve(lambda e: e.tensor_tensor(ROWS[0:1, 0:1024], SCR[0:1, 0:1024],
                                      SCR[0:1, 1024:2048], ALU.mult), rwb, scrb)
        dve(lambda e: e.tensor_scalar(ROWS[0:1, 0:1024], ROWS[0:1, 0:1024],
                                      -1.0, None, ALU.mult), rwb, rwb)
        for nt in range(2):
            mm(pview(0, half=nt), ONESF[0:1, 0:128],
               SCR[0:1, 1024 + nt * 512:1024 + (nt + 1) * 512], pbufs(0, nt),
               scrb + V["ONESF_S"].bufs)
            mm(pview(1, half=nt), ONESF[0:1, 0:128],
               ROWS[0:1, nt * 512:(nt + 1) * 512], pbufs(1, nt),
               rwb + V["ONESF_S"].bufs)
        for dc in range(16):
            dve(lambda e, d=dc: e.tensor_tensor(SQ.ap, xv[d].ap, Pt[0][:],
                                                ALU.mult), SQ.bufs,
                xv[dc].bufs + pbufs(0))
            dve(lambda e, d=dc: e.tensor_tensor(xn_t(d), SQ.ap, Pt[1][:],
                                                ALU.add), xnb, SQ.bufs + pbufs(1))
        return dict(CIN=CIN, COUT=COUT, OCI=OCI, OCO=OCO, cinb=cinb,
                    coutb=coutb, ocib=ocib, ocob=ocob, rows_cin=rows_cin,
                    vbase=vbase, qbase=qbase, xsrc=xsrc, xsrc_buf=xsrc_buf,
                    xdst=xdst, xdst_buf=xdst_buf)

    mk.layer_common = layer
    mk.helpers = dict(mm=mm, pe_tr=pe_tr, act=act, dve=dve, dma=dma, idma=idma,
                      coll=coll, flat_ap=flat_ap, pview=pview, pbufs=pbufs,
                      drbuf=drbuf, xn_t=xn_t, Pt=Pt)


def emit_layer_body(mk, L, ctx):
    nc, prog = mk.nc, mk.P
    V, D = mk.v, mk.d
    h = mk.helpers
    mm, act, dve, dma, idma, coll = (h["mm"], h["act"], h["dve"], h["dma"],
                                     h["idma"], h["coll"])
    pview, pbufs, xn_t, Pt, pe_tr = h["pview"], h["pbufs"], h["xn_t"], h["Pt"], h["pe_tr"]
    flat_ap, drbuf = h["flat_ap"], h["drbuf"]
    b = prog.buf
    ONESBF = V["ONESBF_S"].ap
    ONESF = V["ONESF_S"].ap
    CONSTS = V["CONSTS_S"].ap
    xnb = V["XN"].bufs
    gsb = V["GS"].bufs
    WS = [V["WSa"], V["WSb"]]
    W2 = [V["W2a"], V["W2b"]]
    KST = [V["KSTa"], V["KSTb"]]
    CIN, COUT, OCI, OCO = ctx["CIN"], ctx["COUT"], ctx["OCI"], ctx["OCO"]
    cinb, coutb, ocib, ocob = ctx["cinb"], ctx["coutb"], ctx["ocib"], ctx["ocob"]
    rows_cin, vbase, qbase = ctx["rows_cin"], ctx["vbase"], ctx["qbase"]

    NQH = 8 if L == 0 else 4
    WQ = D["W0Q"] if L == 0 else D["W1Q"]
    WK = D["W0K"] if L == 0 else D["W1K"]
    WV = D["W0V"] if L == 0 else D["W1V"]
    WO = D["W0O"] if L == 0 else D["W1O"]
    kdout = 2048 if L == 0 else 512
    cout_rows = 4 * rows_cin

    def rope_tile(src_view):
        RSH = V["RSH"]
        T = src_view.ap
        R = KST[1].ap
        dma("sp", RSH.ap[0:64, :], T[64:128, :], RSH.bufs, src_view.bufs)
        dma("sp", RSH.ap[64:128, :], T[0:64, :], RSH.bufs, src_view.bufs)
        CS2, SN2 = V["CS2"], V["SN2"]
        for half, op2 in ((0, ALU.subtract), (1, ALU.add)):
            lo, hi = half * 64, half * 64 + 64
            dve(lambda e, l=lo, hh=hi: e.tensor_tensor(
                R[l:hh, :], T[l:hh, :], CS2.ap[l:hh, :], ALU.mult),
                KST[1].bufs, src_view.bufs + CS2.bufs)
            dve(lambda e, l=lo, hh=hi: e.tensor_tensor(
                RSH.ap[l:hh, :], RSH.ap[l:hh, :], SN2.ap[l:hh, :], ALU.mult),
                RSH.bufs, RSH.bufs + SN2.bufs)
            dve(lambda e, l=lo, hh=hi, o=op2: e.tensor_tensor(
                R[l:hh, :], R[l:hh, :], RSH.ap[l:hh, :], o),
                KST[1].bufs, KST[1].bufs + RSH.bufs)

    def proj_to_cin(wdram, dout, base_el, do_rope):
        wre = wdram.rearrange("(dc p) m -> p dc m", p=128)
        for mt in range(dout // 128):
            ws = WS[mt % 2]
            dma("pool", ws.ap.rearrange("p (dc m) -> p dc m", m=128),
                wre[:, :, mt * 128:(mt + 1) * 128], ws.bufs, [])
            pi = 2 + (mt % 2)
            for dc in range(16):
                for nt in range(2):
                    mm(pview(pi, half=nt), ws.ap[:, dc * 128:(dc + 1) * 128],
                       xn_t(dc, nt), pbufs(pi, nt), ws.bufs + xnb,
                       start=(dc == 0), stop=(dc == 15))
            kst = KST[0]
            dve(lambda e, p=pi, k=kst: e.tensor_copy(k.ap, Pt[p][:]),
                kst.bufs, pbufs(pi))
            out_view = kst
            if do_rope:
                rope_tile(kst)
                out_view = KST[1]
            dma("sp", bass.AP(CIN.tensor, base_el + mt * 128 * 1024,
                              [[1024, 128], [1, 1024]]),
                out_view.ap, [cinb], out_view.bufs)

    proj_to_cin(WK, kdout, 0, L == 1)
    for dblk in range(kdout // 512):
        for dc in range(16):
            w2 = W2[dc % 2]
            dma("pool", w2.ap, bass.AP(WV.tensor, dc * 128 * kdout + dblk * 512,
                                       [[kdout, 128], [1, 512]]), w2.bufs, [])
            for tb in range(8):
                mm(pview(tb // 2, half=tb % 2),
                   mk.s["XN"][:, dc * 1024 + tb * 128: dc * 1024 + tb * 128 + 128],
                   w2.ap, pbufs(tb // 2, tb % 2), w2.bufs + xnb,
                   start=(dc == 0), stop=(dc == 15))
        for tb in range(8):
            pk = V["PK"] if tb % 2 == 0 else V["PK2"]
            dve(lambda e, t=tb, k=pk: e.tensor_copy(k.ap[:, 0:512],
                                                    pview(t // 2, half=t % 2)),
                pk.bufs, pbufs(tb // 2, tb % 2))
            dma("sp", bass.AP(CIN.tensor,
                              vbase // 1 + (dblk * 2 * 1024 + tb * 128) * 256,
                              [[256, 128], [1024 * 256, 2], [1, 256]]),
                pk.ap[:, 0:512].rearrange("p (g f) -> p g f", f=256),
                [cinb], pk.bufs)
    proj_to_cin(WQ, 2048, qbase, L == 1)

    coll("AllGather", CIN, COUT, [coutb], [cinb])


def emit_ffn(mk, L, ctx):
    prog, V, D = mk.P, mk.v, mk.d
    h = mk.helpers
    mm, act, dve, dma = h["mm"], h["act"], h["dve"], h["dma"]
    pview, pbufs, xn_t, Pt, pe_tr = h["pview"], h["pbufs"], h["xn_t"], h["Pt"], h["pe_tr"]
    ONESBF = V["ONESBF_S"].ap
    xnb = V["XN"].bufs
    WS = [V["WSa"], V["WSb"]]
    W2 = [V["W2a"], V["W2b"]]
    GT = V["GTA"].ap
    gtb = V["GTA"].bufs
    CBF = mk.s["CBF"]
    cbfb = V["CBF"].bufs
    CB = V["CB"]
    KSTa = mk.s["KSTa"]
    kstb = V["KSTa"].bufs
    E = 8 if L == 0 else 16
    WGD = D["W0G"] if L == 0 else D["G1T"]
    IDENF = V["IDENF_S"].ap
    IDENBF = V["IDENBF_S"].ap
    CONSTS = V["CONSTS_S"].ap

    w2a = V["W2a"]
    dma("pool", w2a.ap[:, 0:16 * E].rearrange("p (dc e) -> p dc e", e=E),
        WGD.rearrange("(dc p) e -> p dc e", p=128), w2a.bufs, [])
    for dc in range(16):
        for nt in range(2):
            mm(pview(0, half=nt, rows=(0, E)), w2a.ap[:, dc * E:(dc + 1) * E],
               xn_t(dc, nt), pbufs(0, nt), w2a.bufs + xnb,
               start=(dc == 0), stop=(dc == 15))
    dve(lambda e: e.tensor_copy(GT[0:E, 0:1024], Pt[0][0:E, :]), gtb, pbufs(0))
    for tt in range(8):
        pe_tr(Pt[1][:, 0:E], GT[0:E, tt * 128:(tt + 1) * 128],
              IDENF[0:E, 0:E], pbufs(1, 0), gtb + V["IDENF_S"].bufs)
        gtm = GT[:, 1024:1024 + E]
        dve(lambda e: e.tensor_copy(gtm, Pt[1][:, 0:E]), gtb, pbufs(1, 0))
        if L == 0:
            sel_in = gtm
        else:
            nm = GT[:, 1040:1041]
            dve(lambda e: e.reduce_max(nm, gtm, AX.X, negate=True), gtb, gtb)
            probs0 = GT[:, 1041:1057]
            sm = GT[:, 1057:1058]
            act(probs0, gtm, AF.Exp, gtb, gtb, bias=nm, accum=sm)
            rc = GT[:, 1058:1059]
            dve(lambda e: e.reciprocal(rc, sm), gtb, gtb)
            probs = GT[:, 1059:1075]
            dve(lambda e: e.tensor_scalar(probs, probs0, rc, None, ALU.mult),
                gtb, gtb)
            sel_in = probs
        Gm = GT[:, 1075:1075 + E * E].rearrange("p (a c) -> p a c", c=E)
        a1 = sel_in.unsqueeze(1).broadcast_to([128, E, E])
        a2 = sel_in.unsqueeze(2).broadcast_to([128, E, E])
        dve(lambda e: e.tensor_tensor(Gm, a1, a2, ALU.is_gt), gtb, gtb)
        rank = GT[:, 1331:1331 + E]
        dve(lambda e: e.tensor_reduce(rank.unsqueeze(2), Gm, AX.X, ALU.add),
            gtb, gtb)
        sel = GT[:, 1347:1347 + E]
        dve(lambda e, k=(3.5 if L == 0 else 5.5): e.tensor_single_scalar(
            sel, rank, k, ALU.is_le), gtb, gtb)
        if L == 0:
            cbv = sel
        else:
            selp = GT[:, 1363:1363 + E]
            dve(lambda e: e.tensor_tensor(selp, sel, probs, ALU.mult), gtb, gtb)
            ts = GT[:, 1379:1380]
            dve(lambda e: e.tensor_reduce(ts, selp, AX.X, ALU.add), gtb, gtb)
            dve(lambda e: e.tensor_scalar(ts, ts, 1e-20, None, ALU.add), gtb, gtb)
            rc2 = GT[:, 1380:1381]
            dve(lambda e: e.reciprocal(rc2, ts), gtb, gtb)
            cbv = GT[:, 1381:1381 + E]
            dve(lambda e: e.tensor_scalar(cbv, selp, rc2, None, ALU.mult),
                gtb, gtb)
        cb_bf = KSTa[:, tt * E:(tt + 1) * E]
        dve(lambda e, s=cbv, t=cb_bf: e.tensor_copy(t, s), kstb, gtb)
        pe_tr(Pt[2][:].bitcast(BF16)[0:E, 0:128], cb_bf,
              IDENBF, pbufs(2, 0), kstb + V["IDENBF_S"].bufs)
        dve(lambda e, t=tt: e.tensor_copy(CBF[0:E, t * 128:(t + 1) * 128],
                                          Pt[2][:].bitcast(BF16)[0:E, 0:128]),
            cbfb, pbufs(2, 0))

    OHS = V["OH8_S"] if L == 0 else V["OH16_S"]
    n_exp = 8 if L == 0 else 18
    lim = os.environ.get("MK_NEXP")
    if lim:
        n_exp = min(n_exp, int(lim))
    for e_i in range(n_exp):
        routed = (L == 0) or (e_i < 16)
        if routed:
            for nt in range(2):
                mm(pview(3, half=nt), OHS.ap[0:E, e_i * 128:(e_i + 1) * 128],
                   CBF[0:E, nt * 512:(nt + 1) * 512], pbufs(3, nt),
                   V["OH8_S" if L == 0 else "OH16_S"].bufs + cbfb)
            for nt in range(2):
                dve(lambda e, n=nt: e.tensor_copy(
                    CB.ap[:, n * 512:(n + 1) * 512], pview(3, half=n)),
                    CB.bufs, pbufs(3, nt))
        if L == 0:
            for hb in range(8):
                ws = WS[hb % 2]
                src = D["W01T"][e_i].rearrange("(dc p) m -> p dc m", p=128)
                dma("pool", ws.ap.rearrange("p (dc m) -> p dc m", m=128),
                    src[:, :, hb * 128:(hb + 1) * 128], ws.bufs, [])
                pi = hb % 2
                for dc in range(16):
                    for nt in range(2):
                        mm(pview(pi, half=nt), ws.ap[:, dc * 128:(dc + 1) * 128],
                           xn_t(dc, nt), pbufs(pi, nt), ws.bufs + xnb,
                           start=(dc == 0), stop=(dc == 15))
                hv = V[f"h{hb}"]
                dve(lambda e, p=pi, o=hv.ap: e.scalar_tensor_tensor(
                    o, Pt[p][:], 0.0, CB.ap, ALU.max, ALU.mult),
                    hv.bufs, pbufs(pi) + CB.bufs)
            for ob in range(8):
                pj = (2, 3) if ob % 2 == 0 else (0, 1)
                for hc in range(8):
                    w2 = W2[hc % 2]
                    dma("pool", w2.ap[:, 0:256],
                        bass.AP(D["W02T"].tensor,
                                e_i * 1024 * 2048 + hc * 128 * 2048 + ob * 256,
                                [[2048, 128], [1, 256]]), w2.bufs, [])
                    hv = V[f"h{hc}"]
                    for m in range(2):
                        for nt in range(2):
                            mm(pview(pj[m], half=nt),
                               w2.ap[:, m * 128:(m + 1) * 128],
                               hv.ap[:, nt * 512:(nt + 1) * 512],
                               pbufs(pj[m], nt), w2.bufs + hv.bufs,
                               start=(hc == 0), stop=(hc == 7))
                for m in range(2):
                    yv = V[f"y{ob * 2 + m}"]
                    if e_i == 0:
                        dve(lambda e, p=pj[m], o=yv.ap: e.tensor_copy(o, Pt[p][:]),
                            yv.bufs, pbufs(pj[m]))
                    else:
                        dve(lambda e, p=pj[m], o=yv.ap: e.tensor_tensor(
                            o, Pt[p][:], o, ALU.add), yv.bufs,
                            pbufs(pj[m]) + yv.bufs)
        else:
            H = e_i - 16
            for gi in range(22):
                ws = WS[gi % 2]
                if routed:
                    src = D["W13T"][e_i].rearrange("(dc p) m -> p dc m", p=128)
                    src = src[:, :, gi * 128:(gi + 1) * 128]
                else:
                    col = H * 1408 + gi * 128 if gi < 11 else \
                        2816 + H * 1408 + (gi - 11) * 128
                    src = D["SW13T"].rearrange("(dc p) m -> p dc m", p=128)
                    src = src[:, :, col:col + 128]
                dma("pool", ws.ap.rearrange("p (dc m) -> p dc m", m=128), src,
                    ws.bufs, [])
                pi = gi % 2
                for dc in range(16):
                    for nt in range(2):
                        mm(pview(pi, half=nt), ws.ap[:, dc * 128:(dc + 1) * 128],
                           xn_t(dc, nt), pbufs(pi, nt), ws.bufs + xnb,
                           start=(dc == 0), stop=(dc == 15))
                if gi < 11:
                    av = V[f"a{gi}"]
                    act(av.ap, Pt[pi][:], AF.Silu, av.bufs, pbufs(pi))
                else:
                    ui = gi - 11
                    av = V[f"a{ui}"]
                    UP = V["UP"]
                    if routed:
                        dve(lambda e, p=pi: e.tensor_tensor(
                            UP.ap, Pt[p][:], CB.ap, ALU.mult),
                            UP.bufs, pbufs(pi) + CB.bufs)
                    else:
                        dve(lambda e, p=pi: e.tensor_copy(UP.ap, Pt[p][:]),
                            UP.bufs, pbufs(pi))
                    dve(lambda e, a=av.ap: e.tensor_tensor(a, a, UP.ap, ALU.mult),
                        av.bufs, av.bufs + UP.bufs)
            for ob in range(8):
                pj = (2, 3) if ob % 2 == 0 else (0, 1)
                for ic in range(11):
                    w2 = W2[ic % 2]
                    if routed:
                        src = bass.AP(D["W12T"].tensor,
                                      e_i * 1408 * 2048 + ic * 128 * 2048 + ob * 256,
                                      [[2048, 128], [1, 256]])
                    else:
                        src = bass.AP(D["SW2T"].tensor,
                                      (H * 1408 + ic * 128) * 2048 + ob * 256,
                                      [[2048, 128], [1, 256]])
                    dma("pool", w2.ap[:, 0:256], src, w2.bufs, [])
                    av = V[f"a{ic}"]
                    for m in range(2):
                        for nt in range(2):
                            mm(pview(pj[m], half=nt),
                               w2.ap[:, m * 128:(m + 1) * 128],
                               av.ap[:, nt * 512:(nt + 1) * 512],
                               pbufs(pj[m], nt), w2.bufs + av.bufs,
                               start=(ic == 0), stop=(ic == 10))
                for m in range(2):
                    yv = V[f"y{ob * 2 + m}"]
                    if e_i == 0:
                        dve(lambda e, p=pj[m], o=yv.ap: e.tensor_copy(o, Pt[p][:]),
                            yv.bufs, pbufs(pj[m]))
                    else:
                        dve(lambda e, p=pj[m], o=yv.ap: e.tensor_tensor(
                            o, Pt[p][:], o, ALU.add), yv.bufs,
                            pbufs(pj[m]) + yv.bufs)
    for dc in range(16):
        yv = V[f"y{dc}"]
        act(mk.s["GS"][:, dc * 1024:(dc + 1) * 1024], yv.ap, AF.Silu,
            V["GS"].bufs, yv.bufs)


def emit_attn(mk, L, ctx):
    prog, V, D = mk.P, mk.v, mk.d
    h = mk.helpers
    mm, act, dve, dma, idma = h["mm"], h["act"], h["dve"], h["dma"], h["idma"]
    pview, pbufs, Pt = h["pview"], h["pbufs"], h["Pt"]
    flat_ap, drbuf = h["flat_ap"], h["drbuf"]
    b = prog.buf
    ONESBF = V["ONESBF_S"].ap
    ONESF = V["ONESF_S"].ap
    CONSTS = V["CONSTS_S"].ap
    XN = mk.s["XN"]
    xnb = V["XN"].bufs
    SCR = mk.s["SCR"]
    scrb = V["SCR"].bufs
    COUT, coutb = ctx["COUT"], ctx["coutb"]
    rows_cin, vbase, qbase = ctx["rows_cin"], ctx["vbase"], ctx["qbase"]
    cout_elems = 4 * rows_cin * 256
    cflat = bass.AP(COUT.tensor, 0, [[1, cout_elems], [1, 1]])
    bflat = bass.AP(D["BROWD"].tensor, 0, [[1, 8 * 1152], [1, 1]])

    if L == 0:
        KP, VA, QL = V["KP"], V["VA"], V["QL"]
        EA, EB2 = V["EA"], V["EB2"]
        BSH = V["BSH"]
        for lp in range(4):
            for r in range(4):
                idma(KP.ap[:, r * 1024:(r + 1) * 1024], cflat, 5 + lp,
                     r * rows_cin * 256, KP.bufs, [coutb])
            for r in range(4):
                idma(QL.ap[:, r * 1024:(r + 1) * 1024], cflat, 5 + lp,
                     r * rows_cin * 256 + qbase, QL.bufs, [coutb])
            for hx in range(2):
                h_l = 2 * lp + hx
                for mi in range(5):
                    bv = V[f"BND{hx}_{mi}"]
                    idma(bv.ap, bflat, mi, h_l * 1152, bv.bufs, [drbuf("BROWD")])
            if lp % 2 == 0:
                g_l = lp // 2
                VG = V["VG"]
                dve(lambda e: e.memset(VA.ap, 1.0), VA.bufs, [])
                for r in range(4):
                    for tc in range(8):
                        kt = r * 8 + tc
                        idma(VG.ap[:, kt * 256:(kt + 1) * 256], cflat, 9 + g_l,
                             r * rows_cin * 256 + vbase + tc * 128 * 256,
                             VG.bufs, [coutb])
                for kt in range(32):
                    dst = VA.ap[:, kt * 260:kt * 260 + 260].rearrange(
                        "p (hh f) -> p hh f", f=65)[:, :, 0:64]
                    srcv = VG.ap[:, kt * 256:(kt + 1) * 256].rearrange(
                        "p (hh f) -> p hh f", f=64)
                    dve(lambda e, d_=dst, s_=srcv: e.tensor_copy(d_, s_),
                        VA.bufs, VG.bufs)
            for gq in range(8):
                for blk in range(16):
                    for kti in range(2):
                        kt = 2 * blk + kti
                        mm(pview(0, half=kti), KP.ap[0:64, kt * 128:(kt + 1) * 128],
                           QL.ap[0:64, gq * 512:(gq + 1) * 512], pbufs(0, kti),
                           KP.bufs + QL.bufs, tp=(0, 0))
                        mm(pview(1, half=kti), KP.ap[64:128, kt * 128:(kt + 1) * 128],
                           QL.ap[64:128, gq * 512:(gq + 1) * 512], pbufs(1, kti),
                           KP.bufs + QL.bufs, tp=(64, 0))
                    m0 = 2 * blk - 4 * gq
                    for hx in range(2):
                        h_l = 2 * lp + hx
                        Ev = EA if hx == 0 else EB2
                        pi = hx
                        if m0 >= 4 or m0 + 1 <= -2:
                            b_ = 0 if m0 >= 4 else 31
                            act(Ev.ap, Pt[pi][:], AF.Exp, Ev.bufs, pbufs(pi),
                                bias=BSH.ap[:, b_ * 8 + h_l:b_ * 8 + h_l + 1])
                        else:
                            for kti in range(2):
                                m = m0 + kti
                                if -1 <= m <= 3:
                                    bv = V[f"BND{hx}_{m + 1}"]
                                    dve(lambda e, p=pi, k=kti, bb=bv.ap:
                                        e.tensor_tensor(
                                            SCR[:, k * 512:(k + 1) * 512],
                                            pview(p, half=k), bb, ALU.add),
                                        scrb, pbufs(pi, kti) + bv.bufs)
                                    act(Ev.ap[:, kti * 512:(kti + 1) * 512],
                                        SCR[:, kti * 512:(kti + 1) * 512],
                                        AF.Exp, Ev.bufs, scrb,
                                        bias=CONSTS[:, 1:2])
                                else:
                                    b_ = 0 if m >= 4 else 31
                                    act(Ev.ap[:, kti * 512:(kti + 1) * 512],
                                        pview(pi, half=kti), AF.Exp, Ev.bufs,
                                        pbufs(pi, kti),
                                        bias=BSH.ap[:, b_ * 8 + h_l:b_ * 8 + h_l + 1])
                    for kti in range(2):
                        kt = 2 * blk + kti
                        for hx in range(2):
                            h_l = 2 * lp + hx
                            Ev = EA if hx == 0 else EB2
                            mm(pview(2, half=hx, rows=(0, 65)),
                               VA.ap[:, kt * 260 + (h_l % 4) * 65:
                                     kt * 260 + (h_l % 4) * 65 + 65],
                               Ev.ap[:, kti * 512:(kti + 1) * 512],
                               pbufs(2, hx), VA.bufs + Ev.bufs,
                               start=(kt == 0), stop=(kt == 31))
                for hx in range(2):
                    h_l = 2 * lp + hx
                    dve(lambda e, x=hx: e.reciprocal(
                        SCR[64:65, 1024:1536], Pt[2][64:65, x * 512:(x + 1) * 512]),
                        scrb, pbufs(2, hx))
                    mm(pview(3, half=hx), ONESF[64:65, 0:128], SCR[64:65, 1024:1536],
                       pbufs(3, hx), scrb + V["ONESF_S"].bufs)
                    dve(lambda e, x=hx: e.tensor_copy(
                        SCR[0:64, 1536:2048], pview(3, half=x, rows=(0, 64))),
                        scrb, pbufs(3, hx))
                    tile = h_l // 2
                    rlo = (h_l % 2) * 64
                    if rlo == 0:
                        dve(lambda e, x=hx, t=tile, g=gq: e.tensor_tensor(
                            XN[0:64, t * 4096 + g * 512:t * 4096 + (g + 1) * 512],
                            Pt[2][0:64, x * 512:(x + 1) * 512],
                            SCR[0:64, 1536:2048], ALU.mult),
                            xnb, pbufs(2, hx) + scrb)
                    else:
                        dve(lambda e, x=hx: e.tensor_tensor(
                            EA.ap[0:64, 0:512],
                            Pt[2][0:64, x * 512:(x + 1) * 512],
                            SCR[0:64, 1536:2048], ALU.mult),
                            EA.bufs, pbufs(2, hx) + scrb)
                        dma("sp",
                            XN[64:128, tile * 4096 + gq * 512:
                               tile * 4096 + (gq + 1) * 512],
                            EA.ap[0:64, 0:512], xnb, EA.bufs)
    else:
        K1, V1, Q1 = V["K1"], V["V1"], V["Q1"]
        E1B, D1B, BR4 = V["E1B"], V["D1B"], V["BR4"]
        MSKL = V["MSKL"]
        LAM = mk.s["LAM_S"]
        for r in range(4):
            idma(K1.ap[:, r * 1024:(r + 1) * 1024], cflat, 11,
                 r * rows_cin * 256, K1.bufs, [coutb])
        for r in range(4):
            for tc in range(8):
                kt = r * 8 + tc
                idma(V1.ap[:, kt * 128:(kt + 1) * 128], cflat, 16,
                     r * rows_cin * 256 + vbase + tc * 128 * 256,
                     V1.bufs, [coutb])
        for ql in range(4):
            for r in range(4):
                idma(Q1.ap[:, r * 1024:(r + 1) * 1024], cflat, 12 + ql,
                     r * rows_cin * 256 + qbase, Q1.bufs, [coutb])
            for gq in range(8):
                kts = list(range(max(0, 4 * gq - 4), 4 * gq + 4))
                nk = len(kts)
                for si, kt in enumerate(kts):
                    mm(pview(0, half=0), K1.ap[0:64, kt * 128:(kt + 1) * 128],
                       Q1.ap[0:64, gq * 512:(gq + 1) * 512], pbufs(0, 0),
                       K1.bufs + Q1.bufs, tp=(0, 0))
                    mm(pview(0, half=1), K1.ap[64:128, kt * 128:(kt + 1) * 128],
                       Q1.ap[64:128, gq * 512:(gq + 1) * 512], pbufs(0, 1),
                       K1.bufs + Q1.bufs, tp=(64, 0))
                    eslice = E1B.ap[:, si * 1024:(si + 1) * 1024]
                    act(eslice, Pt[0][:], AF.Exp, E1B.bufs, pbufs(0),
                        scale=0.125)
                    mi = kt - 4 * gq + 4
                    e3d = eslice.rearrange("p (t f) -> p t f", f=512)
                    mb = MSKL.ap[:, mi * 512:(mi + 1) * 512].unsqueeze(1) \
                        .broadcast_to([128, 2, 512])
                    dve(lambda e, o=e3d, mm_=mb: e.tensor_tensor(o, o, mm_,
                                                                 ALU.mult),
                        E1B.bufs, E1B.bufs + MSKL.bufs)
                    mm(pview(1, half=0, rows=(0, 1)), ONESBF[:, 0:1],
                       E1B.ap[:, si * 1024:si * 1024 + 512], pbufs(1, 0),
                       E1B.bufs + V["ONESBF_S"].bufs,
                       start=(si == 0), stop=(si == nk - 1))
                    mm(pview(1, half=1, rows=(0, 1)), ONESBF[:, 0:1],
                       E1B.ap[:, si * 1024 + 512:(si + 1) * 1024], pbufs(1, 1),
                       E1B.bufs + V["ONESBF_S"].bufs,
                       start=(si == 0), stop=(si == nk - 1))
                dve(lambda e: e.reciprocal(SCR[0:1, 0:512], Pt[1][0:1, 0:512]),
                    scrb, pbufs(1, 0))
                dve(lambda e: e.reciprocal(SCR[0:1, 512:1024],
                                           Pt[1][0:1, 512:1024]), scrb, pbufs(1, 1))
                dve(lambda e, q=ql: e.tensor_scalar(
                    SCR[0:1, 512:1024], SCR[0:1, 512:1024],
                    LAM[0:1, q:q + 1], None, ALU.mult), scrb,
                    scrb + V["LAM_S"].bufs)
                dve(lambda e: e.tensor_copy(BR4.ap[0:1, 0:512], SCR[0:1, 0:512]),
                    BR4.bufs, scrb)
                dve(lambda e: e.tensor_copy(BR4.ap[0:1, 512:1024],
                                            SCR[0:1, 512:1024]), BR4.bufs, scrb)
                mm(pview(3, half=0), ONESBF[0:1, 0:128], BR4.ap[0:1, 0:512],
                   pbufs(3, 0), BR4.bufs + V["ONESBF_S"].bufs)
                mm(pview(3, half=1), ONESBF[0:1, 0:128], BR4.ap[0:1, 512:1024],
                   pbufs(3, 1), BR4.bufs + V["ONESBF_S"].bufs)
                dve(lambda e: e.tensor_copy(BR4.ap[:, 1024:1536], pview(3, half=0)),
                    BR4.bufs, pbufs(3, 0))
                dve(lambda e: e.tensor_copy(BR4.ap[:, 1536:2048], pview(3, half=1)),
                    BR4.bufs, pbufs(3, 1))
                for si, kt in enumerate(kts):
                    dve(lambda e, s=si: e.tensor_tensor(
                        BR4.ap[:, 0:512], E1B.ap[:, s * 1024:s * 1024 + 512],
                        BR4.ap[:, 1024:1536], ALU.mult), BR4.bufs,
                        E1B.bufs + BR4.bufs)
                    dve(lambda e, s=si: e.tensor_tensor(
                        BR4.ap[:, 512:1024],
                        E1B.ap[:, s * 1024 + 512:(s + 1) * 1024],
                        BR4.ap[:, 1536:2048], ALU.mult), BR4.bufs,
                        E1B.bufs + BR4.bufs)
                    dve(lambda e: e.tensor_tensor(
                        BR4.ap[:, 0:512], BR4.ap[:, 0:512], BR4.ap[:, 512:1024],
                        ALU.subtract), BR4.bufs, BR4.bufs)
                    dve(lambda e, s=si: e.tensor_relu(
                        D1B.ap[:, s * 512:(s + 1) * 512], BR4.ap[:, 0:512]),
                        D1B.bufs, BR4.bufs)
                for si, kt in enumerate(kts):
                    mm(pview(2, half=0), V1.ap[:, kt * 128:(kt + 1) * 128],
                       D1B.ap[:, si * 512:(si + 1) * 512], pbufs(2, 0),
                       V1.bufs + D1B.bufs, start=(si == 0), stop=(si == nk - 1))
                    mm(pview(2, half=1, rows=(0, 1)), ONESBF[:, 0:1],
                       D1B.ap[:, si * 512:(si + 1) * 512], pbufs(2, 1),
                       D1B.bufs + V["ONESBF_S"].bufs,
                       start=(si == 0), stop=(si == nk - 1))
                dve(lambda e: e.tensor_scalar(SCR[0:1, 0:512],
                                              Pt[2][0:1, 512:1024], 1e-6,
                                              None, ALU.add), scrb, pbufs(2, 1))
                dve(lambda e: e.reciprocal(SCR[0:1, 0:512], SCR[0:1, 0:512]),
                    scrb, scrb)
                dve(lambda e: e.tensor_copy(BR4.ap[0:1, 0:512], SCR[0:1, 0:512]),
                    BR4.bufs, scrb)
                mm(pview(3, half=0), ONESBF[0:1, 0:128], BR4.ap[0:1, 0:512],
                   pbufs(3, 0), BR4.bufs + V["ONESBF_S"].bufs)
                dve(lambda e: e.tensor_copy(BR4.ap[:, 1024:1536], pview(3, half=0)),
                    BR4.bufs, pbufs(3, 0))
                dve(lambda e, q=ql, g=gq: e.tensor_tensor(
                    XN[:, q * 4096 + g * 512:q * 4096 + (g + 1) * 512],
                    Pt[2][:, 0:512], BR4.ap[:, 1024:1536], ALU.mult),
                    xnb, pbufs(2, 0) + BR4.bufs)

    OCI, OCO, ocib, ocob = ctx["OCI"], ctx["OCO"], ctx["ocib"], ctx["ocob"]
    for t in range(4):
        dma("sp", bass.AP(OCI.tensor, t * 128 * 4096, [[4096, 128], [1, 4096]]),
            XN[:, t * 4096:(t + 1) * 4096], [ocib], xnb)
    h["coll"]("AllGather", OCI, OCO, [ocob], [ocib])
    oflat = bass.AP(OCO.tensor, 0, [[1, 32768 * 256], [1, 1]])
    for i in range(4):
        for dt in range(4):
            idma(XN[:, (i * 4 + dt) * 1024:(i * 4 + dt + 1) * 1024], oflat, 17,
                 i * 8192 * 256 + dt * 128 * 4096, xnb, [ocob])


def emit_combine(mk, L, ctx):
    prog, V, D = mk.P, mk.v, mk.d
    h = mk.helpers
    mm, dve, dma = h["mm"], h["dve"], h["dma"]
    pview, pbufs, xn_t, Pt = h["pview"], h["pbufs"], h["xn_t"], h["Pt"]
    WS = [V["WSa"], V["WSb"]]
    GS = mk.s["GS"]
    gsb = V["GS"].bufs
    xnb = V["XN"].bufs
    WO = D["W0O"] if L == 0 else D["W1O"]
    wre = WO.rearrange("(dc p) m -> p dc m", p=128)
    xsrc, xdst = ctx["xsrc"], ctx["xdst"]
    xsrc_buf, xdst_buf = ctx["xsrc_buf"], ctx["xdst_buf"]
    XC = [V["XC0"], V["XC1"]]
    CT = [V["CT0"], V["CT1"]]
    for mt in range(16):
        xc = XC[mt % 2]
        dma("sp", xc.ap,
            bass.AP(xsrc.tensor, mt * 128 * 1024, [[1024, 128], [1, 1024]]),
            xc.bufs, xsrc_buf)
        ws = WS[mt % 2]
        dma("pool", ws.ap.rearrange("p (dc m) -> p dc m", m=128),
            wre[:, :, mt * 128:(mt + 1) * 128], ws.bufs, [])
        pi = mt % 2
        for dc in range(16):
            for nt in range(2):
                mm(pview(pi, half=nt), ws.ap[:, dc * 128:(dc + 1) * 128],
                   xn_t(dc, nt), pbufs(pi, nt), ws.bufs + xnb,
                   start=(dc == 0), stop=(dc == 15))
        ct = CT[mt % 2]
        dve(lambda e, p=pi, m=mt, c=ct.ap: e.tensor_tensor(
            c, Pt[p][:], GS[:, m * 1024:(m + 1) * 1024], ALU.mult),
            ct.bufs, pbufs(pi) + gsb)
        dve(lambda e, c=ct.ap, x=xc.ap: e.tensor_tensor(c, c, x, ALU.add),
            ct.bufs, ct.bufs + xc.bufs)
        dma("sp", bass.AP(xdst.tensor, mt * 128 * 1024, [[1024, 128], [1, 1024]]),
            ct.ap, [xdst_buf], ct.bufs)


def build_full(stop_after=None, dump_sbuf=(), dump_dram=()):
    nc, prog, mk, st = build_program()
    mk.arena_view("A2", "XC0", 0, F32, 1024)
    mk.arena_view("A2", "XC1", 4096, F32, 1024)
    mk.arena_view("A2", "CT0", 8192, F32, 1024)
    mk.arena_view("A2", "CT1", 12288, F32, 1024)
    mk.finish_views()
    with st:
        with ExitStack() as pst:
            P = [pst.enter_context(nc.psum_tensor(f"P{i}", [128, 1024], F32))
                 for i in range(4)]
            emit_layers(mk, P)
            done = False
            for L in range(2):
                if done:
                    break
                ctx = mk.layer_common(L)
                stages = [(f"L{L}BODY", lambda: emit_layer_body(mk, L, ctx)),
                          (f"L{L}FFN", lambda: emit_ffn(mk, L, ctx)),
                          (f"L{L}ATTN", lambda: emit_attn(mk, L, ctx)),
                          (f"L{L}CMB", lambda: emit_combine(mk, L, ctx))]
                for nm, fn in stages:
                    fn()
                    if stop_after == nm:
                        done = True
                        break
            for nm in sorted(DBG_DRAM):
                ap = mk.d[nm]
                shape = list(ap.tensor.shape)
                o = nc.dram_tensor("DBGD_" + nm, shape, ap.dtype,
                                   kind="ExternalOutput").ap()
                mk.helpers["dma"]("sp", o, ap, [prog.buf("dbgd" + nm)],
                                  [prog.buf("dr:" + nm),
                                   prog.buf("dr:CIN0"), prog.buf("dr:COUT0"),
                                   prog.buf("dr:CIN1"), prog.buf("dr:COUT1"),
                                   prog.buf("dr:OCI0"), prog.buf("dr:OCO0"),
                                   prog.buf("dr:OCI1"), prog.buf("dr:OCO1"),
                                   prog.buf("dr:X1"), prog.buf("dr:BROWD")])
            for nm in dump_sbuf:
                t = mk.s[nm]
                shape = list(t[:].shape)
                o = nc.dram_tensor("DBG_" + nm, shape, t[:].dtype,
                                   kind="ExternalOutput").ap()
                deps = [bb for kk, bb in prog.bufs.items()
                        if kk == "sb:" + nm or kk.startswith(f"ar:{nm}:")]
                mk.helpers["dma"]("sp", o, t[:], [prog.buf("dbg" + nm)], deps)
            prog.finalize_and_emit()
    return nc



_NC_CACHE = {}


def kernel(**inp):
    key = (os.environ.get("MK_STOP"), os.environ.get("MK_DUMP_SBUF", ""),
           os.environ.get("MK_DUMP_DRAM", ""))
    if _NC_CACHE.get("key") != key:
        _NC_CACHE["prog"] = build_full(
            stop_after=key[0] or None,
            dump_sbuf=[x for x in key[1].split(",") if x])
        _NC_CACHE["key"] = key
    nc = _NC_CACHE["prog"]

    x = np.asarray(inp["L0_attn_hidden_states"])
    cosf, sinf = _rope_tables()
    w01t = np.ascontiguousarray(np.transpose(np.asarray(inp["L0_ffn_w1"]), (0, 2, 1)))
    w02t = np.ascontiguousarray(np.transpose(np.asarray(inp["L0_ffn_w2"]), (0, 2, 1)))
    g1t = np.ascontiguousarray(np.asarray(inp["L1_ffn_gate_w"]).T)
    w13t = np.ascontiguousarray(np.transpose(np.asarray(inp["L1_ffn_w13"]), (0, 2, 1)))
    w12t = np.ascontiguousarray(np.transpose(np.asarray(inp["L1_ffn_w2"]), (0, 2, 1)))
    sw13t = np.ascontiguousarray(np.asarray(inp["L1_ffn_sw13"]).T)
    sw2t = np.ascontiguousarray(np.asarray(inp["L1_ffn_sw2"]).T)
    relb = np.asarray(inp["L0_attn_rel_bias"])
    lam = np.asarray(inp["L1_attn_lam"])

    in_maps = []
    for c in range(8):
        s, j = c // 4, c % 4
        C = _host_constants(j)
        xs = np.ascontiguousarray(
            x[s, j * 1024:(j + 1) * 1024, :].T)
        cs = np.ascontiguousarray(cosf[j * 1024:(j + 1) * 1024].T)
        sn = np.ascontiguousarray(sinf[j * 1024:(j + 1) * 1024].T)
        cs2 = np.zeros((128, 1024), np.float32)
        cs2[0:64] = cs
        cs2[64:128] = cs
        sn2 = np.zeros((128, 1024), np.float32)
        sn2[0:64] = sn
        sn2[64:128] = sn
        m = dict(
            X0=xs,
            W0Q=np.asarray(inp["L0_attn_Wq"]), W0K=np.asarray(inp["L0_attn_Wk"]),
            W0V=np.asarray(inp["L0_attn_Wv"]), W0O=np.asarray(inp["L0_attn_Wo"]),
            W0G=np.asarray(inp["L0_ffn_wg"]), W01T=w01t, W02T=w02t,
            RELB=np.ascontiguousarray(relb[:, 8 * j:8 * j + 8]),
            W1Q=np.asarray(inp["L1_attn_Wq"]), W1K=np.asarray(inp["L1_attn_Wk"]),
            W1V=np.asarray(inp["L1_attn_Wv"]), W1O=np.asarray(inp["L1_attn_Wo"]),
            LAMC=np.ascontiguousarray(lam[4 * j:4 * j + 4]).reshape(1, 4),
            G1T=g1t, W13T=w13t, W12T=w12t, SW13T=sw13t, SW2T=sw2t,
            COS2=cs2.astype(BF), SIN2=sn2.astype(BF),
            MSK1D=C["MSK1"], CONSTS=C["CONSTS"],
            ONESBF=C["ONESBF"], ONESF=C["ONESF"],
            IDENF=C["IDENF"], IDENBF=C["IDENBF"],
            OH8=C["OH8"], OH16=C["OH16"],
            IDXALL=C["IDXALL"], OHB=C["OHB"],
        )
        in_maps.append(m)

    res = run_bass_kernel_spmd(nc, in_maps, core_ids=list(range(8)))
    _NC_CACHE["last_results"] = res.results
    out = np.empty((2, 4096, 2048), np.float32)
    for c in range(8):
        s, j = c // 4, c % 4
        out[s, j * 1024:(j + 1) * 1024, :] = res.results[c]["OUT"].T
    return out


# version 02 (final)
# speedup vs baseline: 1.3909x; 1.3909x over previous
import math
import os
import numpy as np
import ml_dtypes
from contextlib import ExitStack

import concourse.bass as bass
import concourse.mybir as mybir
from concourse.bass_utils import run_bass_kernel_spmd
from concourse.bass_core_rust import IndirectOffsetOnAxis

DBG_DRAM = set(os.environ.get("MK_DUMP_DRAM", "").split(",")) - {""}

F32 = mybir.dt.float32
BF16 = mybir.dt.bfloat16
I32 = mybir.dt.int32
AF = mybir.ActivationFunctionType
ALU = mybir.AluOpType
AX = mybir.AxisListType


ENGS = ("pe", "act", "dve", "pool", "sp")


class Buf:
    __slots__ = ("name", "writer", "readers")

    def __init__(self, name):
        self.name = name
        self.writer = None
        self.readers = []


class Op:
    __slots__ = ("idx", "eng", "emit", "deps", "kind", "signaled", "sig")

    def __init__(self, idx, eng, emit, deps, kind):
        self.idx = idx
        self.eng = eng
        self.emit = emit
        self.deps = deps
        self.kind = kind
        self.signaled = False
        self.sig = None


class Prog:

    def __init__(self, nc):
        self.nc = nc
        self.ops = []
        self.bufs = {}

    def buf(self, name):
        b = self.bufs.get(name)
        if b is None:
            b = self.bufs[name] = Buf(name)
        return b

    def op(self, eng, emit, reads=(), writes=(), kind="c"):
        i = len(self.ops)
        deps = set()
        for b in reads:
            if b.writer is not None:
                deps.add(b.writer)
        for b in writes:
            if b.writer is not None:
                deps.add(b.writer)
            deps.update(b.readers)
        deps.discard(i)
        o = Op(i, eng, emit, deps, kind)
        self.ops.append(o)
        for b in reads:
            b.readers.append(i)
        for b in writes:
            b.writer = i
            b.readers = []
        return o

    def dma(self, eng, dst_ap, src_ap, reads=(), writes=()):
        return self.op(eng, lambda e: e.dma_start(dst_ap, src_ap),
                       reads=reads, writes=writes, kind="d")

    def finalize_and_emit(self):
        nc = self.nc
        ops = self.ops
        for o in ops:
            kept = set()
            for d in o.deps:
                p = ops[d]
                if p.kind == "c" and p.eng == "pe" and o.eng == "pe":
                    continue
                kept.add(d)
                if p.kind == "c":
                    p.signaled = True
            o.deps = kept
        NL = 4
        dma_engs = sorted({o.eng for o in ops if o.kind == "d"})
        sems = {}
        with ExitStack() as st:
            for e in ENGS:
                sems[("c", e)] = st.enter_context(nc.semaphore(f"sc_{e}"))
            for e in dma_engs:
                for l in range(NL):
                    sems[("d", e, l)] = st.enter_context(
                        nc.semaphore(f"sd_{e}{l}"))
            sems[("coll", "pool")] = st.enter_context(nc.semaphore("s_coll"))
            counts = {k: 0 for k in sems}
            dma_idx = {e: 0 for e in dma_engs}
            for o in ops:
                if o.kind == "d":
                    i = dma_idx[o.eng]
                    dma_idx[o.eng] += 1
                    lane = i % NL
                    o.sig = (("d", o.eng, lane), 16 * (i // NL + 1))
                    counts[("d", o.eng, lane)] = o.sig[1]
                elif o.kind == "coll":
                    counts[("coll", "pool")] += 1
                    o.sig = (("coll", "pool"), counts[("coll", "pool")])
                elif o.signaled:
                    counts[("c", o.eng)] += 1
                    o.sig = (("c", o.eng), counts[("c", o.eng)])
            final_counts = dict(counts)

            per_eng = {e: [o for o in ops if o.eng == e] for e in ENGS}
            nc_eng = {"pe": nc.tensor, "act": nc.scalar, "dve": nc.vector,
                      "pool": nc.gpsimd, "sp": nc.sync}

            def run_engine(e, engine):
                vclock = {}
                ndma = 0
                for o in per_eng[e]:
                    if o.kind == "d":
                        i = ndma
                        ndma += 1
                        if i >= NL:
                            key = ("d", e, i % NL)
                            v = 16 * (i // NL)
                            if vclock.get(key, 0) < v:
                                engine.wait_ge(sems[key], v)
                                vclock[key] = v
                    for d in sorted(o.deps):
                        p = ops[d]
                        if p.sig is None:
                            continue
                        key, val = p.sig
                        if vclock.get(key, 0) >= val:
                            continue
                        engine.wait_ge(sems[key], val)
                        vclock[key] = val
                    inst = o.emit(engine)
                    if o.sig is not None:
                        key, _ = o.sig
                        n = 16 if key[0] == "d" else 1
                        inst.then_inc(sems[key], n)
                if e == "sp":
                    for key, val in final_counts.items():
                        if val > 0 and vclock.get(key, 0) < val:
                            engine.wait_ge(sems[key], val)

            with nc.Block() as block:
                block.tensor(lambda eng: run_engine("pe", eng))
                block.scalar(lambda eng: run_engine("act", eng))
                block.vector(lambda eng: run_engine("dve", eng))
                block.gpsimd(lambda eng: run_engine("pool", eng))
                block.sync(lambda eng: run_engine("sp", eng))



def _bucket_ref(d):
    NUM_BUCKETS, MAX_DIST = 32, 128
    rp = max(d, 0)
    max_exact = NUM_BUCKETS // 2
    if rp < max_exact:
        return rp
    v = np.float32(np.log(np.float32(rp) / np.float32(max_exact)))
    v = np.float32(v / np.float32(math.log(MAX_DIST / max_exact)))
    v = np.float32(v * np.float32(NUM_BUCKETS - max_exact))
    b = max_exact + int(v)
    return min(b, NUM_BUCKETS - 1)


def _host_constants(j):
    C = {}
    ohb = np.zeros((32, 1152), np.float32)
    for i in range(1152):
        ohb[_bucket_ref(i - 512), i] = 1.0
    C["OHB"] = ohb
    p = np.arange(128, dtype=np.int32)
    cols = []
    for m in (-1, 0, 1, 2, 3):
        delta = -128 * m
        cols.append(delta + 512 - p)
    for lp in range(4):
        cols.append((j * 512 + lp * 128 + p) * 1024)
    for g in range(2):
        cols.append(((2 * j + g) * 1024 + p) * 260 if False else ((2 * j + g) * 1024 + p) * 256)
    cols.append((j * 128 + p) * 1024)
    for ql in range(4):
        cols.append(((4 * j + ql) * 128 + p) * 1024)
    cols.append(((j // 2) * 1024 + p) * 256 + (j % 2) * 128)
    cols.append(p * 4096 + j * 1024)
    idx = np.stack(cols, axis=1).astype(np.int32)
    C["IDXALL"] = np.ascontiguousarray(idx)
    qf = np.arange(512)
    kk = np.arange(128)
    msk = np.zeros((8, 128, 512), np.float32)
    for mi, m in enumerate(range(-4, 4)):
        d = (-128 * m) + qf[None, :] - kk[:, None]
        msk[mi] = ((d >= 0) & (d <= 511)).astype(np.float32)
    C["MSK1"] = msk.astype(ml_dtypes.bfloat16)
    C["CONSTS"] = np.tile(np.array(
        [[1e-5, -40.0, 1.0, 0.0, 1.0 / 2048.0, 1e-6, 1e-20, 0.0]], np.float32),
        (128, 1))
    C["ONESBF"] = np.ones((128, 128), ml_dtypes.bfloat16)
    C["ONESF"] = np.ones((128, 128), np.float32)
    C["IDENF"] = np.eye(128, dtype=np.float32)
    C["IDENBF"] = np.eye(128, dtype=ml_dtypes.bfloat16)
    oh8 = np.zeros((8, 8 * 128), np.float32)
    for e in range(8):
        oh8[e, e * 128:(e + 1) * 128] = 1.0
    C["OH8"] = oh8.astype(ml_dtypes.bfloat16)
    oh16 = np.zeros((16, 16 * 128), np.float32)
    for e in range(16):
        oh16[e, e * 128:(e + 1) * 128] = 1.0
    C["OH16"] = oh16.astype(ml_dtypes.bfloat16)
    return C


def _rope_tables(S=4096, HEAD_DIM=128, THETA=10000.0):
    inv_freq = 1.0 / THETA ** (np.arange(0, HEAD_DIM, 2, dtype=np.float32) / HEAD_DIM)
    freqs = np.arange(S, dtype=np.float32)[:, None] * inv_freq[None, :]
    return np.cos(freqs), np.sin(freqs)


BF = ml_dtypes.bfloat16


class View:
    __slots__ = ("ap", "bufs")

    def __init__(self, ap, bufs):
        self.ap = ap
        self.bufs = bufs


class MK:
    def __init__(self, nc, prog):
        self.nc = nc
        self.P = prog
        self.d = {}
        self.s = {}
        self.v = {}
        self._arena_ranges = {}
        self._pending = []

    def dram(self, name, shape, dtype, kind="ExternalInput"):
        ap = self.nc.dram_tensor(name, shape, dtype, kind=kind).ap()
        self.d[name] = ap
        return ap

    def sbuf(self, st, name, shape, dtype):
        t = st.enter_context(self.nc.sbuf_tensor(name, shape, dtype))
        self.s[name] = t
        v = View(t[:], [self.P.buf("sb:" + name)])
        self.v[name] = v
        return v

    def arena_view(self, arena, name, off_bytes, dtype, cols):
        esz = 4 if dtype in (F32, I32) else 2
        t = self.s[arena]
        base = t[:].bitcast(dtype) if dtype != F32 else t[:]
        ap = base[:, off_bytes // esz: off_bytes // esz + cols]
        nbytes = cols * esz
        self._arena_ranges.setdefault(arena, []).append(
            (off_bytes, off_bytes + nbytes, name))
        v = View(ap, None)
        self.v[name] = v
        self._pending.append((arena, name, off_bytes, off_bytes + nbytes, v))
        return v

    def finish_views(self):
        for arena, name, s0, e0, v in self._pending:
            bufs = [self.P.buf(f"ar:{arena}:{name}")]
            for (s1, e1, other) in self._arena_ranges[arena]:
                if other != name and s1 < e0 and s0 < e1:
                    bufs.append(self.P.buf(f"ar:{arena}:{other}"))
            v.bufs = bufs
        self._pending = []


def build_program():
    nc = bass.Bass("TRN2", target_bir_lowering=False, debug=False)
    prog = Prog(nc)
    mk = MK(nc, prog)
    d = mk.dram

    d("X0", [2048, 1024], F32)
    d("W0Q", [2048, 2048], F32); d("W0K", [2048, 2048], F32)
    d("W0V", [2048, 2048], F32); d("W0O", [2048, 2048], F32)
    d("W0G", [2048, 8], F32)
    d("W01T", [8, 2048, 1024], F32); d("W02T", [8, 1024, 2048], F32)
    d("RELB", [32, 8], F32); d("OHB", [32, 1152], F32)
    d("W1Q", [2048, 2048], F32); d("W1K", [2048, 512], F32)
    d("W1V", [2048, 512], F32); d("W1O", [2048, 2048], F32)
    d("LAMC", [1, 4], F32); d("G1T", [2048, 16], F32)
    d("W13T", [16, 2048, 2816], F32); d("W12T", [16, 1408, 2048], F32)
    d("SW13T", [2048, 5632], F32); d("SW2T", [2816, 2048], F32)
    d("COS2", [128, 1024], BF16); d("SIN2", [128, 1024], BF16)
    d("MSK1D", [8, 128, 512], BF16)
    d("CONSTS", [128, 8], F32)
    d("ONESBF", [128, 128], BF16); d("ONESF", [128, 128], F32)
    d("IDENF", [128, 128], F32); d("IDENBF", [128, 128], BF16)
    d("OH8", [8, 1024], BF16); d("OH16", [16, 2048], BF16)
    d("IDXALL", [128, 18], I32)
    d("OUT", [2048, 1024], F32, kind="ExternalOutput")
    ik = lambda n: "Internal"
    d("X1", [2048, 1024], F32, kind=ik("X1"))
    d("BROWD", [8, 1152], BF16, kind=ik("BROWD"))
    d("C0I", [24576, 256], BF16, kind=ik("C0I"))
    d("C0O", [4 * 24576, 256], BF16, kind=ik("C0O"))
    d("C1I", [12288, 256], BF16, kind=ik("C1I"))
    d("C1O", [4 * 12288, 256], BF16, kind=ik("C1O"))
    d("OC0I", [8192, 256], BF16, kind=ik("OC0I"))
    d("OC0O", [32768, 256], BF16, kind=ik("OC0O"))
    d("OC1I", [8192, 256], BF16, kind=ik("OC1I"))
    d("OC1O", [32768, 256], BF16, kind=ik("OC1O"))

    st = ExitStack()
    mk.st = st
    sb = lambda name, shape, dt: mk.sbuf(st, name, shape, dt)

    sb("XN", [128, 16384], BF16)
    sb("GS", [128, 16384], BF16)
    sb("AR", [128, 16384], F32)
    sb("A2", [128, 8192], F32)
    sb("WSa", [128, 2048], BF16); sb("WSb", [128, 2048], BF16)
    sb("W2a", [128, 512], BF16); sb("W2b", [128, 512], BF16)
    sb("KSTa", [128, 1024], BF16); sb("KSTb", [128, 1024], BF16)
    sb("RSH", [128, 1024], BF16)
    sb("CS2", [128, 1024], BF16); sb("SN2", [128, 1024], BF16)
    sb("BSH", [128, 256], F32)
    sb("ROWS", [128, 1024], F32)
    sb("CONSTS_S", [128, 8], F32)
    sb("ONESBF_S", [128, 128], BF16); sb("ONESF_S", [128, 128], F32)
    sb("IDENF_S", [128, 128], F32); sb("IDENBF_S", [128, 128], BF16)
    sb("OH8_S", [8, 1024], BF16); sb("OH16_S", [16, 2048], BF16)
    sb("IDXS", [128, 18], I32)
    sb("LAM_S", [1, 4], F32)
    sb("SCR", [128, 2048], F32)
    sb("CBF", [16, 1024], BF16)
    sb("CB", [128, 1024], F32)

    av = mk.arena_view
    for dc in range(16):
        av("AR", f"x{dc}", dc * 4096, F32, 1024)
    for dc in range(16):
        av("AR", f"y{dc}", dc * 4096, F32, 1024)
    av("AR", "KP", 0, BF16, 4096)
    av("AR", "VA", 8192, BF16, 8320)
    av("AR", "QL", 24832, BF16, 4096)
    av("AR", "VG", 47360, BF16, 8192)
    av("AR", "EA", 33024, BF16, 1024)
    av("AR", "EB2", 35072, BF16, 1024)
    for h in range(2):
        for m in range(5):
            av("AR", f"BND{h}_{m}", 37120 + (h * 5 + m) * 1024, BF16, 512)
    av("AR", "K1", 0, BF16, 4096)
    av("AR", "V1", 8192, BF16, 4096)
    av("AR", "Q1", 16384, BF16, 4096)
    av("AR", "E1B", 24576, BF16, 8192)
    av("AR", "D1B", 40960, BF16, 4096)
    av("AR", "BR4", 49152, BF16, 2048)
    av("AR", "MSKL", 53248, BF16, 4096)
    av("A2", "SQ", 0, F32, 1024)
    av("A2", "HH", 4096, BF16, 1024); av("A2", "HL", 6144, BF16, 1024)
    av("A2", "H2H", 8192, BF16, 1024); av("A2", "H2L", 10240, BF16, 1024)
    for hb in range(8):
        av("A2", f"h{hb}", hb * 2048, BF16, 1024)
    for gi in range(11):
        av("A2", f"a{gi}", gi * 2048, BF16, 1024)
    av("A2", "UP", 22528, BF16, 1024)
    av("A2", "GTA", 16384, F32, 1536)
    av("A2", "BROWV", 28672, BF16, 1152)
    av("A2", "PK", 24576, BF16, 1024)
    av("A2", "PK2", 26624, BF16, 1024)
    mk.finish_views()

    b = prog.buf
    return nc, prog, mk, st


def emit_layers(mk, P):
    nc, prog = mk.nc, mk.P
    V = mk.v
    D = mk.d
    b = prog.buf

    Pt = P
    pb = [[b(f"ps{i}a"), b(f"ps{i}b")] for i in range(4)]

    def pview(i, half=None, rows=None, cols=None):
        ap = Pt[i][:]
        if half is not None:
            ap = Pt[i][:, half * 512:(half + 1) * 512]
        if cols is not None:
            ap = Pt[i][:, cols[0]:cols[1]]
        if rows is not None:
            lo, hi = rows
            ap = ap[lo:hi]
        return ap

    def pbufs(i, half=None):
        if half is None:
            return pb[i]
        return [pb[i][half]]

    dbuf = {}
    def drbuf(name):
        return b("dr:" + name)

    def mm(out_ap, lhsT, rhs, wb, rb, start=True, stop=True, tp=None):
        prog.op("pe", lambda e, o=out_ap, l=lhsT, r=rhs, s=start, t=stop, q=tp:
                e.matmul(o, l, r, start=s, stop=t, tile_position=q,
                         skip_group_check=True),
                reads=rb, writes=wb)

    def pe_tr(out_ap, in_ap, iden, wb, rb):
        prog.op("pe", lambda e, o=out_ap, i=in_ap, d=iden: e.transpose(o, i, d),
                reads=rb, writes=wb)

    def act(eng_out, in_ap, func, wb, rb, bias=0.0, scale=1.0, accum=None):
        prog.op("act", lambda e, o=eng_out, i=in_ap, f=func, bi=bias, sc=scale,
                a=accum: e.activation(o, i, f, bias=bi, scale=sc, accum_out=a),
                reads=rb, writes=wb)

    def dve(fn, wb, rb, eng="dve"):
        prog.op(eng, fn, reads=rb, writes=wb)

    def dma(eng, dst, src, wb, rb):
        prog.dma(eng, dst, src, reads=rb, writes=wb)

    def idma(dst, in_ap, idxcol, eoff, wb, rb):
        idx = V["IDXS"].ap[:, idxcol:idxcol + 1]
        prog.op("pool", lambda e, d_=dst, i_=in_ap, x_=idx, o_=eoff:
                e.indirect_dma_start(d_, None, i_, IndirectOffsetOnAxis(x_, 0),
                                     element_offset=o_),
                reads=rb + [b("sb:IDXS")], writes=wb, kind="d")

    def coll(kind, ins, outs, wb, rb):
        groups = [[0, 1, 2, 3], [4, 5, 6, 7]]
        prog.op("pool", lambda e, k=kind, i=ins, o=outs, g=groups:
                e.collective_compute(k, ALU.bypass, replica_groups=g,
                                     ins=[i.opt()], outs=[o.opt()]),
                reads=rb, writes=wb, kind="coll")

    def flat_ap(name, n):
        return bass.AP(D[name].tensor, 0, [[1, n], [1, 1]])

    ONESBF = V["ONESBF_S"].ap
    ONESF = V["ONESF_S"].ap
    CONSTS = V["CONSTS_S"].ap
    XN = mk.s["XN"][:]
    GS = mk.s["GS"][:]
    xnb = V["XN"].bufs
    gsb = V["GS"].bufs

    def xn_t(dc, nt=None):
        if nt is None:
            return XN.bitcast(BF16)[:, dc * 1024:(dc + 1) * 1024] if False else \
                mk.s["XN"][:, dc * 1024:(dc + 1) * 1024]
        return mk.s["XN"][:, dc * 1024 + nt * 512: dc * 1024 + (nt + 1) * 512]

    for nm, dst in (("CONSTS", "CONSTS_S"), ("ONESBF", "ONESBF_S"),
                    ("ONESF", "ONESF_S"), ("IDENF", "IDENF_S"),
                    ("IDENBF", "IDENBF_S"), ("OH8", "OH8_S"),
                    ("OH16", "OH16_S"), ("IDXALL", "IDXS"),
                    ("LAMC", "LAM_S"), ("COS2", "CS2"), ("SIN2", "SN2")):
        dma("sp", V[dst].ap, D[nm], V[dst].bufs, [])
    dma("sp", V["MSKL"].ap.rearrange("p (m f) -> p m f", f=512),
        D["MSK1D"].rearrange("m p f -> p m f"), V["MSKL"].bufs, [])

    dma("pool", mk.s["W2a"][0:32, 0:8], D["RELB"], V["W2a"].bufs, [])
    dma("pool", mk.s["WSa"][0:32, 0:1152], D["OHB"], V["WSa"].bufs, [])
    mm(pview(0, cols=(0, 512), rows=(0, 8)), mk.s["W2a"][0:32, 0:8],
       mk.s["WSa"][0:32, 0:512], pbufs(0, 0), V["W2a"].bufs + V["WSa"].bufs)
    mm(pview(0, cols=(512, 1024), rows=(0, 8)), mk.s["W2a"][0:32, 0:8],
       mk.s["WSa"][0:32, 512:1024], pbufs(0, 1), V["W2a"].bufs + V["WSa"].bufs)
    mm(pview(1, cols=(0, 128), rows=(0, 8)), mk.s["W2a"][0:32, 0:8],
       mk.s["WSa"][0:32, 1024:1152], pbufs(1, 0), V["W2a"].bufs + V["WSa"].bufs)
    BROWV = V["BROWV"]
    dve(lambda e: e.tensor_copy(BROWV.ap[0:8, 0:1024], pview(0, rows=(0, 8))),
        BROWV.bufs, pbufs(0))
    dve(lambda e: e.tensor_copy(BROWV.ap[0:8, 1024:1152],
                                pview(1, cols=(0, 128), rows=(0, 8))),
        BROWV.bufs, pbufs(1))
    dma("sp", D["BROWD"], BROWV.ap[0:8, :], [drbuf("BROWD")], BROWV.bufs)
    dma("sp", mk.s["SCR"][0:1, 0:256],
        bass.AP(D["RELB"].tensor, 0, [[256, 1], [1, 256]]), V["SCR"].bufs, [])
    mm(pview(2, cols=(0, 256)), ONESF[0:1, 0:128], mk.s["SCR"][0:1, 0:256],
       pbufs(2, 0), V["SCR"].bufs + V["ONESF_S"].bufs)
    dve(lambda e: e.tensor_scalar(V["BSH"].ap, pview(2, cols=(0, 256)), -40.0,
                                  None, ALU.add),
        V["BSH"].bufs, pbufs(2, 0))

    def layer(L):
        xsrc = D["X0"] if L == 0 else D["X1"]
        xsrc_buf = [drbuf("X1")] if L == 1 else []
        xdst = D["X1"] if L == 0 else D["OUT"]
        xdst_buf = drbuf("X1") if L == 0 else drbuf("OUT")
        CIN = D["C0I"] if L == 0 else D["C1I"]
        COUT = D["C0O"] if L == 0 else D["C1O"]
        OCI = D["OC0I"] if L == 0 else D["OC1I"]
        OCO = D["OC0O"] if L == 0 else D["OC1O"]
        cinb, coutb = drbuf("CIN%d" % L), drbuf("COUT%d" % L)
        ocib, ocob = drbuf("OCI%d" % L), drbuf("OCO%d" % L)
        rows_cin = 24576 if L == 0 else 12288
        vbase = 8192 * 256 if L == 0 else 2048 * 256
        qbase = 16384 * 256 if L == 0 else 4096 * 256

        xv = [V[f"x{dc}"] for dc in range(16)]
        for dc in range(16):
            dma("sp", xv[dc].ap,
                bass.AP(xsrc.tensor, dc * 128 * 1024, [[1024, 128], [1, 1024]]),
                xv[dc].bufs, xsrc_buf)
        SQ, HH, HL, H2H, H2L = V["SQ"], V["HH"], V["HL"], V["H2H"], V["H2L"]
        for dc in range(16):
            act(SQ.ap, xv[dc].ap, AF.Square, SQ.bufs, xv[dc].bufs)
            dve(lambda e, d=dc: e.tensor_copy(HH.ap, xv[d].ap), HH.bufs, xv[dc].bufs)
            dve(lambda e, d=dc: e.tensor_tensor(HL.ap, xv[d].ap, HH.ap, ALU.subtract),
                HL.bufs, xv[dc].bufs + HH.bufs)
            dve(lambda e: e.tensor_copy(H2H.ap, SQ.ap), H2H.bufs, SQ.bufs)
            dve(lambda e: e.tensor_tensor(H2L.ap, SQ.ap, H2H.ap, ALU.subtract),
                H2L.bufs, SQ.bufs + H2H.bufs)
            for nt in range(2):
                for si, src in enumerate((HH, HL)):
                    mm(pview(2, half=nt, rows=(0, 1)), ONESBF[:, 0:1],
                       src.ap[:, nt * 512:(nt + 1) * 512], pbufs(2, nt),
                       src.bufs + V["ONESBF_S"].bufs,
                       start=(dc == 0 and si == 0), stop=(dc == 15 and si == 1))
                for si, src in enumerate((H2H, H2L)):
                    mm(pview(3, half=nt, rows=(0, 1)), ONESBF[:, 0:1],
                       src.ap[:, nt * 512:(nt + 1) * 512], pbufs(3, nt),
                       src.bufs + V["ONESBF_S"].bufs,
                       start=(dc == 0 and si == 0), stop=(dc == 15 and si == 1))
        SCR = mk.s["SCR"]
        scrb = V["SCR"].bufs
        ROWS = mk.s["ROWS"]
        rwb = V["ROWS"].bufs
        dve(lambda e: e.tensor_scalar(SCR[0:1, 0:1024], Pt[2][0:1, :],
                                      1.0 / 2048.0, None, ALU.mult), scrb, pbufs(2))
        dve(lambda e: e.tensor_scalar(SCR[0:1, 1024:2048], Pt[3][0:1, :],
                                      1.0 / 2048.0, None, ALU.mult), scrb, pbufs(3))
        dve(lambda e: e.tensor_tensor(ROWS[0:1, 0:1024], SCR[0:1, 0:1024],
                                      SCR[0:1, 0:1024], ALU.mult), rwb, scrb)
        dve(lambda e: e.tensor_tensor(SCR[0:1, 1024:2048], SCR[0:1, 1024:2048],
                                      ROWS[0:1, 0:1024], ALU.subtract), scrb,
            scrb + rwb)
        act(SCR[0:1, 1024:2048], SCR[0:1, 1024:2048], AF.Ln, scrb, scrb,
            bias=CONSTS[0:1, 0:1])
        act(SCR[0:1, 1024:2048], SCR[0:1, 1024:2048], AF.Exp, scrb, scrb,
            scale=-0.5)
        dve(lambda e: e.tensor_tensor(ROWS[0:1, 0:1024], SCR[0:1, 0:1024],
                                      SCR[0:1, 1024:2048], ALU.mult), rwb, scrb)
        dve(lambda e: e.tensor_scalar(ROWS[0:1, 0:1024], ROWS[0:1, 0:1024],
                                      -1.0, None, ALU.mult), rwb, rwb)
        for nt in range(2):
            mm(pview(0, half=nt), ONESF[0:1, 0:128],
               SCR[0:1, 1024 + nt * 512:1024 + (nt + 1) * 512], pbufs(0, nt),
               scrb + V["ONESF_S"].bufs)
            mm(pview(1, half=nt), ONESF[0:1, 0:128],
               ROWS[0:1, nt * 512:(nt + 1) * 512], pbufs(1, nt),
               rwb + V["ONESF_S"].bufs)
        for dc in range(16):
            dve(lambda e, d=dc: e.tensor_tensor(SQ.ap, xv[d].ap, Pt[0][:],
                                                ALU.mult), SQ.bufs,
                xv[dc].bufs + pbufs(0))
            dve(lambda e, d=dc: e.tensor_tensor(xn_t(d), SQ.ap, Pt[1][:],
                                                ALU.add), xnb, SQ.bufs + pbufs(1))
        return dict(CIN=CIN, COUT=COUT, OCI=OCI, OCO=OCO, cinb=cinb,
                    coutb=coutb, ocib=ocib, ocob=ocob, rows_cin=rows_cin,
                    vbase=vbase, qbase=qbase, xsrc=xsrc, xsrc_buf=xsrc_buf,
                    xdst=xdst, xdst_buf=xdst_buf)

    mk.layer_common = layer
    mk.helpers = dict(mm=mm, pe_tr=pe_tr, act=act, dve=dve, dma=dma, idma=idma,
                      coll=coll, flat_ap=flat_ap, pview=pview, pbufs=pbufs,
                      drbuf=drbuf, xn_t=xn_t, Pt=Pt)


def emit_layer_body(mk, L, ctx):
    nc, prog = mk.nc, mk.P
    V, D = mk.v, mk.d
    h = mk.helpers
    mm, act, dve, dma, idma, coll = (h["mm"], h["act"], h["dve"], h["dma"],
                                     h["idma"], h["coll"])
    pview, pbufs, xn_t, Pt, pe_tr = h["pview"], h["pbufs"], h["xn_t"], h["Pt"], h["pe_tr"]
    flat_ap, drbuf = h["flat_ap"], h["drbuf"]
    b = prog.buf
    ONESBF = V["ONESBF_S"].ap
    ONESF = V["ONESF_S"].ap
    CONSTS = V["CONSTS_S"].ap
    xnb = V["XN"].bufs
    gsb = V["GS"].bufs
    WS = [V["WSa"], V["WSb"]]
    W2 = [V["W2a"], V["W2b"]]
    KST = [V["KSTa"], V["KSTb"]]
    CIN, COUT, OCI, OCO = ctx["CIN"], ctx["COUT"], ctx["OCI"], ctx["OCO"]
    cinb, coutb, ocib, ocob = ctx["cinb"], ctx["coutb"], ctx["ocib"], ctx["ocob"]
    rows_cin, vbase, qbase = ctx["rows_cin"], ctx["vbase"], ctx["qbase"]

    NQH = 8 if L == 0 else 4
    WQ = D["W0Q"] if L == 0 else D["W1Q"]
    WK = D["W0K"] if L == 0 else D["W1K"]
    WV = D["W0V"] if L == 0 else D["W1V"]
    WO = D["W0O"] if L == 0 else D["W1O"]
    kdout = 2048 if L == 0 else 512
    cout_rows = 4 * rows_cin

    def rope_tile(src_view):
        RSH = V["RSH"]
        T = src_view.ap
        R = KST[1].ap
        dma("sp", RSH.ap[0:64, :], T[64:128, :], RSH.bufs, src_view.bufs)
        dma("sp", RSH.ap[64:128, :], T[0:64, :], RSH.bufs, src_view.bufs)
        CS2, SN2 = V["CS2"], V["SN2"]
        for half, op2 in ((0, ALU.subtract), (1, ALU.add)):
            lo, hi = half * 64, half * 64 + 64
            dve(lambda e, l=lo, hh=hi: e.tensor_tensor(
                R[l:hh, :], T[l:hh, :], CS2.ap[l:hh, :], ALU.mult),
                KST[1].bufs, src_view.bufs + CS2.bufs)
            dve(lambda e, l=lo, hh=hi: e.tensor_tensor(
                RSH.ap[l:hh, :], RSH.ap[l:hh, :], SN2.ap[l:hh, :], ALU.mult),
                RSH.bufs, RSH.bufs + SN2.bufs)
            dve(lambda e, l=lo, hh=hi, o=op2: e.tensor_tensor(
                R[l:hh, :], R[l:hh, :], RSH.ap[l:hh, :], o),
                KST[1].bufs, KST[1].bufs + RSH.bufs)

    def proj_to_cin(wdram, dout, base_el, do_rope):
        wre = wdram.rearrange("(dc p) m -> p dc m", p=128)
        for mt in range(dout // 128):
            ws = WS[mt % 2]
            dma("pool", ws.ap.rearrange("p (dc m) -> p dc m", m=128),
                wre[:, :, mt * 128:(mt + 1) * 128], ws.bufs, [])
            pi = 2 + (mt % 2)
            for dc in range(16):
                for nt in range(2):
                    mm(pview(pi, half=nt), ws.ap[:, dc * 128:(dc + 1) * 128],
                       xn_t(dc, nt), pbufs(pi, nt), ws.bufs + xnb,
                       start=(dc == 0), stop=(dc == 15))
            kst = KST[0]
            dve(lambda e, p=pi, k=kst: e.tensor_copy(k.ap, Pt[p][:]),
                kst.bufs, pbufs(pi))
            out_view = kst
            if do_rope:
                rope_tile(kst)
                out_view = KST[1]
            dma("sp", bass.AP(CIN.tensor, base_el + mt * 128 * 1024,
                              [[1024, 128], [1, 1024]]),
                out_view.ap, [cinb], out_view.bufs)

    proj_to_cin(WK, kdout, 0, L == 1)
    for dblk in range(kdout // 512):
        for dc in range(16):
            w2 = W2[dc % 2]
            dma("pool", w2.ap, bass.AP(WV.tensor, dc * 128 * kdout + dblk * 512,
                                       [[kdout, 128], [1, 512]]), w2.bufs, [])
            for tb in range(8):
                mm(pview(tb // 2, half=tb % 2),
                   mk.s["XN"][:, dc * 1024 + tb * 128: dc * 1024 + tb * 128 + 128],
                   w2.ap, pbufs(tb // 2, tb % 2), w2.bufs + xnb,
                   start=(dc == 0), stop=(dc == 15))
        for tb in range(8):
            pk = V["PK"] if tb % 2 == 0 else V["PK2"]
            dve(lambda e, t=tb, k=pk: e.tensor_copy(k.ap[:, 0:512],
                                                    pview(t // 2, half=t % 2)),
                pk.bufs, pbufs(tb // 2, tb % 2))
            dma("sp", bass.AP(CIN.tensor,
                              vbase // 1 + (dblk * 2 * 1024 + tb * 128) * 256,
                              [[256, 128], [1024 * 256, 2], [1, 256]]),
                pk.ap[:, 0:512].rearrange("p (g f) -> p g f", f=256),
                [cinb], pk.bufs)
    proj_to_cin(WQ, 2048, qbase, L == 1)

    coll("AllGather", CIN, COUT, [coutb], [cinb])


def emit_ffn(mk, L, ctx):
    prog, V, D = mk.P, mk.v, mk.d
    h = mk.helpers
    mm, act, dve, dma = h["mm"], h["act"], h["dve"], h["dma"]
    pview, pbufs, xn_t, Pt, pe_tr = h["pview"], h["pbufs"], h["xn_t"], h["Pt"], h["pe_tr"]
    ONESBF = V["ONESBF_S"].ap
    xnb = V["XN"].bufs
    WS = [V["WSa"], V["WSb"]]
    W2 = [V["W2a"], V["W2b"]]
    GT = V["GTA"].ap
    gtb = V["GTA"].bufs
    CBF = mk.s["CBF"]
    cbfb = V["CBF"].bufs
    CB = V["CB"]
    KSTa = mk.s["KSTa"]
    kstb = V["KSTa"].bufs
    E = 8 if L == 0 else 16
    WGD = D["W0G"] if L == 0 else D["G1T"]
    IDENF = V["IDENF_S"].ap
    IDENBF = V["IDENBF_S"].ap
    CONSTS = V["CONSTS_S"].ap

    w2a = V["W2a"]
    dma("pool", w2a.ap[:, 0:16 * E].rearrange("p (dc e) -> p dc e", e=E),
        WGD.rearrange("(dc p) e -> p dc e", p=128), w2a.bufs, [])
    for dc in range(16):
        for nt in range(2):
            mm(pview(0, half=nt, rows=(0, E)), w2a.ap[:, dc * E:(dc + 1) * E],
               xn_t(dc, nt), pbufs(0, nt), w2a.bufs + xnb,
               start=(dc == 0), stop=(dc == 15))
    dve(lambda e: e.tensor_copy(GT[0:E, 0:1024], Pt[0][0:E, :]), gtb, pbufs(0))
    for tt in range(8):
        pe_tr(Pt[1][:, 0:E], GT[0:E, tt * 128:(tt + 1) * 128],
              IDENF[0:E, 0:E], pbufs(1, 0), gtb + V["IDENF_S"].bufs)
        gtm = GT[:, 1024:1024 + E]
        dve(lambda e: e.tensor_copy(gtm, Pt[1][:, 0:E]), gtb, pbufs(1, 0))
        if L == 0:
            sel_in = gtm
        else:
            nm = GT[:, 1040:1041]
            dve(lambda e: e.reduce_max(nm, gtm, AX.X, negate=True), gtb, gtb)
            probs0 = GT[:, 1041:1057]
            sm = GT[:, 1057:1058]
            act(probs0, gtm, AF.Exp, gtb, gtb, bias=nm, accum=sm)
            rc = GT[:, 1058:1059]
            dve(lambda e: e.reciprocal(rc, sm), gtb, gtb)
            probs = GT[:, 1059:1075]
            dve(lambda e: e.tensor_scalar(probs, probs0, rc, None, ALU.mult),
                gtb, gtb)
            sel_in = probs
        Gm = GT[:, 1075:1075 + E * E].rearrange("p (a c) -> p a c", c=E)
        a1 = sel_in.unsqueeze(1).broadcast_to([128, E, E])
        a2 = sel_in.unsqueeze(2).broadcast_to([128, E, E])
        dve(lambda e: e.tensor_tensor(Gm, a1, a2, ALU.is_gt), gtb, gtb)
        rank = GT[:, 1331:1331 + E]
        dve(lambda e: e.tensor_reduce(rank.unsqueeze(2), Gm, AX.X, ALU.add),
            gtb, gtb)
        sel = GT[:, 1347:1347 + E]
        dve(lambda e, k=(3.5 if L == 0 else 5.5): e.tensor_single_scalar(
            sel, rank, k, ALU.is_le), gtb, gtb)
        if L == 0:
            cbv = sel
        else:
            selp = GT[:, 1363:1363 + E]
            dve(lambda e: e.tensor_tensor(selp, sel, probs, ALU.mult), gtb, gtb)
            ts = GT[:, 1379:1380]
            dve(lambda e: e.tensor_reduce(ts, selp, AX.X, ALU.add), gtb, gtb)
            dve(lambda e: e.tensor_scalar(ts, ts, 1e-20, None, ALU.add), gtb, gtb)
            rc2 = GT[:, 1380:1381]
            dve(lambda e: e.reciprocal(rc2, ts), gtb, gtb)
            cbv = GT[:, 1381:1381 + E]
            dve(lambda e: e.tensor_scalar(cbv, selp, rc2, None, ALU.mult),
                gtb, gtb)
        cb_bf = KSTa[:, tt * E:(tt + 1) * E]
        dve(lambda e, s=cbv, t=cb_bf: e.tensor_copy(t, s), kstb, gtb)
        pe_tr(Pt[2][:].bitcast(BF16)[0:E, 0:128], cb_bf,
              IDENBF, pbufs(2, 0), kstb + V["IDENBF_S"].bufs)
        dve(lambda e, t=tt: e.tensor_copy(CBF[0:E, t * 128:(t + 1) * 128],
                                          Pt[2][:].bitcast(BF16)[0:E, 0:128]),
            cbfb, pbufs(2, 0))

    OHS = V["OH8_S"] if L == 0 else V["OH16_S"]
    n_exp = 8 if L == 0 else 18
    lim = os.environ.get("MK_NEXP")
    if lim:
        n_exp = min(n_exp, int(lim))
    for e_i in range(n_exp):
        routed = (L == 0) or (e_i < 16)
        if routed:
            for nt in range(2):
                mm(pview(3, half=nt), OHS.ap[0:E, e_i * 128:(e_i + 1) * 128],
                   CBF[0:E, nt * 512:(nt + 1) * 512], pbufs(3, nt),
                   V["OH8_S" if L == 0 else "OH16_S"].bufs + cbfb)
            for nt in range(2):
                dve(lambda e, n=nt: e.tensor_copy(
                    CB.ap[:, n * 512:(n + 1) * 512], pview(3, half=n)),
                    CB.bufs, pbufs(3, nt))
        if L == 0:
            for hb in range(8):
                ws = WS[hb % 2]
                src = D["W01T"][e_i].rearrange("(dc p) m -> p dc m", p=128)
                dma("pool", ws.ap.rearrange("p (dc m) -> p dc m", m=128),
                    src[:, :, hb * 128:(hb + 1) * 128], ws.bufs, [])
                pi = hb % 2
                for dc in range(16):
                    for nt in range(2):
                        mm(pview(pi, half=nt), ws.ap[:, dc * 128:(dc + 1) * 128],
                           xn_t(dc, nt), pbufs(pi, nt), ws.bufs + xnb,
                           start=(dc == 0), stop=(dc == 15))
                hv = V[f"h{hb}"]
                dve(lambda e, p=pi, o=hv.ap: e.scalar_tensor_tensor(
                    o, Pt[p][:], 0.0, CB.ap, ALU.max, ALU.mult),
                    hv.bufs, pbufs(pi) + CB.bufs)
            for ob in range(8):
                pj = (2, 3) if ob % 2 == 0 else (0, 1)
                wsl = WS[ob % 2]
                dma("pool", wsl.ap.rearrange("p (hc m) -> p hc m", m=256),
                    bass.AP(D["W02T"].tensor, e_i * 1024 * 2048 + ob * 256,
                            [[2048, 128], [128 * 2048, 8], [1, 256]]),
                    wsl.bufs, [])
                for hc in range(8):
                    hv = V[f"h{hc}"]
                    for m in range(2):
                        for nt in range(2):
                            mm(pview(pj[m], half=nt),
                               wsl.ap[:, hc * 256 + m * 128:
                                      hc * 256 + (m + 1) * 128],
                               hv.ap[:, nt * 512:(nt + 1) * 512],
                               pbufs(pj[m], nt), wsl.bufs + hv.bufs,
                               start=(hc == 0), stop=(hc == 7))
                for m in range(2):
                    yv = V[f"y{ob * 2 + m}"]
                    if e_i == 0:
                        dve(lambda e, p=pj[m], o=yv.ap: e.tensor_copy(o, Pt[p][:]),
                            yv.bufs, pbufs(pj[m]))
                    else:
                        dve(lambda e, p=pj[m], o=yv.ap: e.tensor_tensor(
                            o, Pt[p][:], o, ALU.add), yv.bufs,
                            pbufs(pj[m]) + yv.bufs)
        else:
            H = e_i - 16
            for gi in range(22):
                ws = WS[gi % 2]
                if routed:
                    src = D["W13T"][e_i].rearrange("(dc p) m -> p dc m", p=128)
                    src = src[:, :, gi * 128:(gi + 1) * 128]
                else:
                    col = H * 1408 + gi * 128 if gi < 11 else \
                        2816 + H * 1408 + (gi - 11) * 128
                    src = D["SW13T"].rearrange("(dc p) m -> p dc m", p=128)
                    src = src[:, :, col:col + 128]
                dma("pool", ws.ap.rearrange("p (dc m) -> p dc m", m=128), src,
                    ws.bufs, [])
                pi = gi % 2
                for dc in range(16):
                    for nt in range(2):
                        mm(pview(pi, half=nt), ws.ap[:, dc * 128:(dc + 1) * 128],
                           xn_t(dc, nt), pbufs(pi, nt), ws.bufs + xnb,
                           start=(dc == 0), stop=(dc == 15))
                if gi < 11:
                    av = V[f"a{gi}"]
                    act(av.ap, Pt[pi][:], AF.Silu, av.bufs, pbufs(pi))
                else:
                    ui = gi - 11
                    av = V[f"a{ui}"]
                    UP = V["UP"]
                    if routed:
                        dve(lambda e, p=pi: e.tensor_tensor(
                            UP.ap, Pt[p][:], CB.ap, ALU.mult),
                            UP.bufs, pbufs(pi) + CB.bufs)
                    else:
                        dve(lambda e, p=pi: e.tensor_copy(UP.ap, Pt[p][:]),
                            UP.bufs, pbufs(pi))
                    dve(lambda e, a=av.ap: e.tensor_tensor(a, a, UP.ap, ALU.mult),
                        av.bufs, av.bufs + UP.bufs)
            for ob in range(8):
                pj = (2, 3) if ob % 2 == 0 else (0, 1)
                base = e_i * 1408 * 2048 if routed else H * 1408 * 2048
                wt = D["W12T"].tensor if routed else D["SW2T"].tensor
                for half_i, (ic0, nic) in enumerate(((0, 6), (6, 5))):
                    wsl = WS[half_i]
                    dma("pool",
                        wsl.ap[:, 0:nic * 256].rearrange("p (ic m) -> p ic m",
                                                         m=256),
                        bass.AP(wt, base + ic0 * 128 * 2048 + ob * 256,
                                [[2048, 128], [128 * 2048, nic], [1, 256]]),
                        wsl.bufs, [])
                for ic in range(11):
                    wsl = WS[0] if ic < 6 else WS[1]
                    icr = ic if ic < 6 else ic - 6
                    av = V[f"a{ic}"]
                    for m in range(2):
                        for nt in range(2):
                            mm(pview(pj[m], half=nt),
                               wsl.ap[:, icr * 256 + m * 128:
                                      icr * 256 + (m + 1) * 128],
                               av.ap[:, nt * 512:(nt + 1) * 512],
                               pbufs(pj[m], nt), wsl.bufs + av.bufs,
                               start=(ic == 0), stop=(ic == 10))
                for m in range(2):
                    yv = V[f"y{ob * 2 + m}"]
                    if e_i == 0:
                        dve(lambda e, p=pj[m], o=yv.ap: e.tensor_copy(o, Pt[p][:]),
                            yv.bufs, pbufs(pj[m]))
                    else:
                        dve(lambda e, p=pj[m], o=yv.ap: e.tensor_tensor(
                            o, Pt[p][:], o, ALU.add), yv.bufs,
                            pbufs(pj[m]) + yv.bufs)
    for dc in range(16):
        yv = V[f"y{dc}"]
        act(mk.s["GS"][:, dc * 1024:(dc + 1) * 1024], yv.ap, AF.Silu,
            V["GS"].bufs, yv.bufs)


def emit_attn(mk, L, ctx):
    prog, V, D = mk.P, mk.v, mk.d
    h = mk.helpers
    mm, act, dve, dma, idma = h["mm"], h["act"], h["dve"], h["dma"], h["idma"]
    pview, pbufs, Pt = h["pview"], h["pbufs"], h["Pt"]
    flat_ap, drbuf = h["flat_ap"], h["drbuf"]
    b = prog.buf
    ONESBF = V["ONESBF_S"].ap
    ONESF = V["ONESF_S"].ap
    CONSTS = V["CONSTS_S"].ap
    XN = mk.s["XN"]
    xnb = V["XN"].bufs
    SCR = mk.s["SCR"]
    scrb = V["SCR"].bufs
    COUT, coutb = ctx["COUT"], ctx["coutb"]
    rows_cin, vbase, qbase = ctx["rows_cin"], ctx["vbase"], ctx["qbase"]
    cout_elems = 4 * rows_cin * 256
    cflat = bass.AP(COUT.tensor, 0, [[1, cout_elems], [1, 1]])
    bflat = bass.AP(D["BROWD"].tensor, 0, [[1, 8 * 1152], [1, 1]])

    if L == 0:
        KP, VA, QL = V["KP"], V["VA"], V["QL"]
        EA, EB2 = V["EA"], V["EB2"]
        BSH = V["BSH"]
        for lp in range(4):
            for r in range(4):
                idma(KP.ap[:, r * 1024:(r + 1) * 1024], cflat, 5 + lp,
                     r * rows_cin * 256, KP.bufs, [coutb])
            for r in range(4):
                idma(QL.ap[:, r * 1024:(r + 1) * 1024], cflat, 5 + lp,
                     r * rows_cin * 256 + qbase, QL.bufs, [coutb])
            for hx in range(2):
                h_l = 2 * lp + hx
                for mi in range(5):
                    bv = V[f"BND{hx}_{mi}"]
                    idma(bv.ap, bflat, mi, h_l * 1152, bv.bufs, [drbuf("BROWD")])
            if lp % 2 == 0:
                g_l = lp // 2
                VG = V["VG"]
                dve(lambda e: e.memset(VA.ap, 1.0), VA.bufs, [])
                for r in range(4):
                    for tc in range(8):
                        kt = r * 8 + tc
                        idma(VG.ap[:, kt * 256:(kt + 1) * 256], cflat, 9 + g_l,
                             r * rows_cin * 256 + vbase + tc * 128 * 256,
                             VG.bufs, [coutb])
                for kt in range(32):
                    dst = VA.ap[:, kt * 260:kt * 260 + 260].rearrange(
                        "p (hh f) -> p hh f", f=65)[:, :, 0:64]
                    srcv = VG.ap[:, kt * 256:(kt + 1) * 256].rearrange(
                        "p (hh f) -> p hh f", f=64)
                    dve(lambda e, d_=dst, s_=srcv: e.tensor_copy(d_, s_),
                        VA.bufs, VG.bufs)
            for gq in range(8):
                for blk in range(16):
                    for kti in range(2):
                        kt = 2 * blk + kti
                        mm(pview(0, half=kti), KP.ap[0:64, kt * 128:(kt + 1) * 128],
                           QL.ap[0:64, gq * 512:(gq + 1) * 512], pbufs(0, kti),
                           KP.bufs + QL.bufs, tp=(0, 0))
                        mm(pview(1, half=kti), KP.ap[64:128, kt * 128:(kt + 1) * 128],
                           QL.ap[64:128, gq * 512:(gq + 1) * 512], pbufs(1, kti),
                           KP.bufs + QL.bufs, tp=(64, 0))
                    m0 = 2 * blk - 4 * gq
                    for hx in range(2):
                        h_l = 2 * lp + hx
                        Ev = EA if hx == 0 else EB2
                        pi = hx
                        if m0 >= 4 or m0 + 1 <= -2:
                            b_ = 0 if m0 >= 4 else 31
                            act(Ev.ap, Pt[pi][:], AF.Exp, Ev.bufs, pbufs(pi),
                                bias=BSH.ap[:, b_ * 8 + h_l:b_ * 8 + h_l + 1])
                        else:
                            for kti in range(2):
                                m = m0 + kti
                                if -1 <= m <= 3:
                                    bv = V[f"BND{hx}_{m + 1}"]
                                    dve(lambda e, p=pi, k=kti, bb=bv.ap:
                                        e.tensor_tensor(
                                            SCR[:, k * 512:(k + 1) * 512],
                                            pview(p, half=k), bb, ALU.add),
                                        scrb, pbufs(pi, kti) + bv.bufs)
                                    act(Ev.ap[:, kti * 512:(kti + 1) * 512],
                                        SCR[:, kti * 512:(kti + 1) * 512],
                                        AF.Exp, Ev.bufs, scrb,
                                        bias=CONSTS[:, 1:2])
                                else:
                                    b_ = 0 if m >= 4 else 31
                                    act(Ev.ap[:, kti * 512:(kti + 1) * 512],
                                        pview(pi, half=kti), AF.Exp, Ev.bufs,
                                        pbufs(pi, kti),
                                        bias=BSH.ap[:, b_ * 8 + h_l:b_ * 8 + h_l + 1])
                    for kti in range(2):
                        kt = 2 * blk + kti
                        for hx in range(2):
                            h_l = 2 * lp + hx
                            Ev = EA if hx == 0 else EB2
                            mm(pview(2, half=hx, rows=(0, 65)),
                               VA.ap[:, kt * 260 + (h_l % 4) * 65:
                                     kt * 260 + (h_l % 4) * 65 + 65],
                               Ev.ap[:, kti * 512:(kti + 1) * 512],
                               pbufs(2, hx), VA.bufs + Ev.bufs,
                               start=(kt == 0), stop=(kt == 31))
                for hx in range(2):
                    h_l = 2 * lp + hx
                    dve(lambda e, x=hx: e.reciprocal(
                        SCR[64:65, 1024:1536], Pt[2][64:65, x * 512:(x + 1) * 512]),
                        scrb, pbufs(2, hx))
                    mm(pview(3, half=hx), ONESF[64:65, 0:128], SCR[64:65, 1024:1536],
                       pbufs(3, hx), scrb + V["ONESF_S"].bufs)
                    dve(lambda e, x=hx: e.tensor_copy(
                        SCR[0:64, 1536:2048], pview(3, half=x, rows=(0, 64))),
                        scrb, pbufs(3, hx))
                    tile = h_l // 2
                    rlo = (h_l % 2) * 64
                    if rlo == 0:
                        dve(lambda e, x=hx, t=tile, g=gq: e.tensor_tensor(
                            XN[0:64, t * 4096 + g * 512:t * 4096 + (g + 1) * 512],
                            Pt[2][0:64, x * 512:(x + 1) * 512],
                            SCR[0:64, 1536:2048], ALU.mult),
                            xnb, pbufs(2, hx) + scrb)
                    else:
                        dve(lambda e, x=hx: e.tensor_tensor(
                            EA.ap[0:64, 0:512],
                            Pt[2][0:64, x * 512:(x + 1) * 512],
                            SCR[0:64, 1536:2048], ALU.mult),
                            EA.bufs, pbufs(2, hx) + scrb)
                        dma("sp",
                            XN[64:128, tile * 4096 + gq * 512:
                               tile * 4096 + (gq + 1) * 512],
                            EA.ap[0:64, 0:512], xnb, EA.bufs)
    else:
        K1, V1, Q1 = V["K1"], V["V1"], V["Q1"]
        E1B, D1B, BR4 = V["E1B"], V["D1B"], V["BR4"]
        MSKL = V["MSKL"]
        LAM = mk.s["LAM_S"]
        for r in range(4):
            idma(K1.ap[:, r * 1024:(r + 1) * 1024], cflat, 11,
                 r * rows_cin * 256, K1.bufs, [coutb])
        for r in range(4):
            for tc in range(8):
                kt = r * 8 + tc
                idma(V1.ap[:, kt * 128:(kt + 1) * 128], cflat, 16,
                     r * rows_cin * 256 + vbase + tc * 128 * 256,
                     V1.bufs, [coutb])
        for ql in range(4):
            for r in range(4):
                idma(Q1.ap[:, r * 1024:(r + 1) * 1024], cflat, 12 + ql,
                     r * rows_cin * 256 + qbase, Q1.bufs, [coutb])
            for gq in range(8):
                kts = list(range(max(0, 4 * gq - 4), 4 * gq + 4))
                nk = len(kts)
                for si, kt in enumerate(kts):
                    mm(pview(0, half=0), K1.ap[0:64, kt * 128:(kt + 1) * 128],
                       Q1.ap[0:64, gq * 512:(gq + 1) * 512], pbufs(0, 0),
                       K1.bufs + Q1.bufs, tp=(0, 0))
                    mm(pview(0, half=1), K1.ap[64:128, kt * 128:(kt + 1) * 128],
                       Q1.ap[64:128, gq * 512:(gq + 1) * 512], pbufs(0, 1),
                       K1.bufs + Q1.bufs, tp=(64, 0))
                    eslice = E1B.ap[:, si * 1024:(si + 1) * 1024]
                    act(eslice, Pt[0][:], AF.Exp, E1B.bufs, pbufs(0),
                        scale=0.125)
                    mi = kt - 4 * gq + 4
                    e3d = eslice.rearrange("p (t f) -> p t f", f=512)
                    mb = MSKL.ap[:, mi * 512:(mi + 1) * 512].unsqueeze(1) \
                        .broadcast_to([128, 2, 512])
                    dve(lambda e, o=e3d, mm_=mb: e.tensor_tensor(o, o, mm_,
                                                                 ALU.mult),
                        E1B.bufs, E1B.bufs + MSKL.bufs)
                    mm(pview(1, half=0, rows=(0, 1)), ONESBF[:, 0:1],
                       E1B.ap[:, si * 1024:si * 1024 + 512], pbufs(1, 0),
                       E1B.bufs + V["ONESBF_S"].bufs,
                       start=(si == 0), stop=(si == nk - 1))
                    mm(pview(1, half=1, rows=(0, 1)), ONESBF[:, 0:1],
                       E1B.ap[:, si * 1024 + 512:(si + 1) * 1024], pbufs(1, 1),
                       E1B.bufs + V["ONESBF_S"].bufs,
                       start=(si == 0), stop=(si == nk - 1))
                dve(lambda e: e.reciprocal(SCR[0:1, 0:512], Pt[1][0:1, 0:512]),
                    scrb, pbufs(1, 0))
                dve(lambda e: e.reciprocal(SCR[0:1, 512:1024],
                                           Pt[1][0:1, 512:1024]), scrb, pbufs(1, 1))
                dve(lambda e, q=ql: e.tensor_scalar(
                    SCR[0:1, 512:1024], SCR[0:1, 512:1024],
                    LAM[0:1, q:q + 1], None, ALU.mult), scrb,
                    scrb + V["LAM_S"].bufs)
                dve(lambda e: e.tensor_copy(BR4.ap[0:1, 0:512], SCR[0:1, 0:512]),
                    BR4.bufs, scrb)
                dve(lambda e: e.tensor_copy(BR4.ap[0:1, 512:1024],
                                            SCR[0:1, 512:1024]), BR4.bufs, scrb)
                mm(pview(3, half=0), ONESBF[0:1, 0:128], BR4.ap[0:1, 0:512],
                   pbufs(3, 0), BR4.bufs + V["ONESBF_S"].bufs)
                mm(pview(3, half=1), ONESBF[0:1, 0:128], BR4.ap[0:1, 512:1024],
                   pbufs(3, 1), BR4.bufs + V["ONESBF_S"].bufs)
                dve(lambda e: e.tensor_copy(BR4.ap[:, 1024:1536], pview(3, half=0)),
                    BR4.bufs, pbufs(3, 0))
                dve(lambda e: e.tensor_copy(BR4.ap[:, 1536:2048], pview(3, half=1)),
                    BR4.bufs, pbufs(3, 1))
                for si, kt in enumerate(kts):
                    dve(lambda e, s=si: e.tensor_tensor(
                        BR4.ap[:, 0:512], E1B.ap[:, s * 1024:s * 1024 + 512],
                        BR4.ap[:, 1024:1536], ALU.mult), BR4.bufs,
                        E1B.bufs + BR4.bufs)
                    dve(lambda e, s=si: e.tensor_tensor(
                        BR4.ap[:, 512:1024],
                        E1B.ap[:, s * 1024 + 512:(s + 1) * 1024],
                        BR4.ap[:, 1536:2048], ALU.mult), BR4.bufs,
                        E1B.bufs + BR4.bufs)
                    dve(lambda e: e.tensor_tensor(
                        BR4.ap[:, 0:512], BR4.ap[:, 0:512], BR4.ap[:, 512:1024],
                        ALU.subtract), BR4.bufs, BR4.bufs)
                    dve(lambda e, s=si: e.tensor_relu(
                        D1B.ap[:, s * 512:(s + 1) * 512], BR4.ap[:, 0:512]),
                        D1B.bufs, BR4.bufs)
                for si, kt in enumerate(kts):
                    mm(pview(2, half=0), V1.ap[:, kt * 128:(kt + 1) * 128],
                       D1B.ap[:, si * 512:(si + 1) * 512], pbufs(2, 0),
                       V1.bufs + D1B.bufs, start=(si == 0), stop=(si == nk - 1))
                    mm(pview(2, half=1, rows=(0, 1)), ONESBF[:, 0:1],
                       D1B.ap[:, si * 512:(si + 1) * 512], pbufs(2, 1),
                       D1B.bufs + V["ONESBF_S"].bufs,
                       start=(si == 0), stop=(si == nk - 1))
                dve(lambda e: e.tensor_scalar(SCR[0:1, 0:512],
                                              Pt[2][0:1, 512:1024], 1e-6,
                                              None, ALU.add), scrb, pbufs(2, 1))
                dve(lambda e: e.reciprocal(SCR[0:1, 0:512], SCR[0:1, 0:512]),
                    scrb, scrb)
                dve(lambda e: e.tensor_copy(BR4.ap[0:1, 0:512], SCR[0:1, 0:512]),
                    BR4.bufs, scrb)
                mm(pview(3, half=0), ONESBF[0:1, 0:128], BR4.ap[0:1, 0:512],
                   pbufs(3, 0), BR4.bufs + V["ONESBF_S"].bufs)
                dve(lambda e: e.tensor_copy(BR4.ap[:, 1024:1536], pview(3, half=0)),
                    BR4.bufs, pbufs(3, 0))
                dve(lambda e, q=ql, g=gq: e.tensor_tensor(
                    XN[:, q * 4096 + g * 512:q * 4096 + (g + 1) * 512],
                    Pt[2][:, 0:512], BR4.ap[:, 1024:1536], ALU.mult),
                    xnb, pbufs(2, 0) + BR4.bufs)

    OCI, OCO, ocib, ocob = ctx["OCI"], ctx["OCO"], ctx["ocib"], ctx["ocob"]
    for t in range(4):
        dma("sp", bass.AP(OCI.tensor, t * 128 * 4096, [[4096, 128], [1, 4096]]),
            XN[:, t * 4096:(t + 1) * 4096], [ocib], xnb)
    h["coll"]("AllGather", OCI, OCO, [ocob], [ocib])
    oflat = bass.AP(OCO.tensor, 0, [[1, 32768 * 256], [1, 1]])
    for i in range(4):
        for dt in range(4):
            idma(XN[:, (i * 4 + dt) * 1024:(i * 4 + dt + 1) * 1024], oflat, 17,
                 i * 8192 * 256 + dt * 128 * 4096, xnb, [ocob])


def emit_combine(mk, L, ctx):
    prog, V, D = mk.P, mk.v, mk.d
    h = mk.helpers
    mm, dve, dma = h["mm"], h["dve"], h["dma"]
    pview, pbufs, xn_t, Pt = h["pview"], h["pbufs"], h["xn_t"], h["Pt"]
    WS = [V["WSa"], V["WSb"]]
    GS = mk.s["GS"]
    gsb = V["GS"].bufs
    xnb = V["XN"].bufs
    WO = D["W0O"] if L == 0 else D["W1O"]
    wre = WO.rearrange("(dc p) m -> p dc m", p=128)
    xsrc, xdst = ctx["xsrc"], ctx["xdst"]
    xsrc_buf, xdst_buf = ctx["xsrc_buf"], ctx["xdst_buf"]
    XC = [V["XC0"], V["XC1"]]
    CT = [V["CT0"], V["CT1"]]
    for mt in range(16):
        xc = XC[mt % 2]
        dma("sp", xc.ap,
            bass.AP(xsrc.tensor, mt * 128 * 1024, [[1024, 128], [1, 1024]]),
            xc.bufs, xsrc_buf)
        ws = WS[mt % 2]
        dma("pool", ws.ap.rearrange("p (dc m) -> p dc m", m=128),
            wre[:, :, mt * 128:(mt + 1) * 128], ws.bufs, [])
        pi = mt % 2
        for dc in range(16):
            for nt in range(2):
                mm(pview(pi, half=nt), ws.ap[:, dc * 128:(dc + 1) * 128],
                   xn_t(dc, nt), pbufs(pi, nt), ws.bufs + xnb,
                   start=(dc == 0), stop=(dc == 15))
        ct = CT[mt % 2]
        dve(lambda e, p=pi, m=mt, c=ct.ap: e.tensor_tensor(
            c, Pt[p][:], GS[:, m * 1024:(m + 1) * 1024], ALU.mult),
            ct.bufs, pbufs(pi) + gsb)
        dve(lambda e, c=ct.ap, x=xc.ap: e.tensor_tensor(c, c, x, ALU.add),
            ct.bufs, ct.bufs + xc.bufs)
        dma("sp", bass.AP(xdst.tensor, mt * 128 * 1024, [[1024, 128], [1, 1024]]),
            ct.ap, [xdst_buf], ct.bufs)


def build_full(stop_after=None, dump_sbuf=(), dump_dram=()):
    nc, prog, mk, st = build_program()
    mk.arena_view("A2", "XC0", 0, F32, 1024)
    mk.arena_view("A2", "XC1", 4096, F32, 1024)
    mk.arena_view("A2", "CT0", 8192, F32, 1024)
    mk.arena_view("A2", "CT1", 12288, F32, 1024)
    mk.finish_views()
    with st:
        with ExitStack() as pst:
            P = [pst.enter_context(nc.psum_tensor(f"P{i}", [128, 1024], F32))
                 for i in range(4)]
            emit_layers(mk, P)
            done = False
            for L in range(2):
                if done:
                    break
                ctx = mk.layer_common(L)
                stages = [(f"L{L}BODY", lambda: emit_layer_body(mk, L, ctx)),
                          (f"L{L}FFN", lambda: emit_ffn(mk, L, ctx)),
                          (f"L{L}ATTN", lambda: emit_attn(mk, L, ctx)),
                          (f"L{L}CMB", lambda: emit_combine(mk, L, ctx))]
                for nm, fn in stages:
                    fn()
                    if stop_after == nm:
                        done = True
                        break
            for nm in sorted(DBG_DRAM):
                ap = mk.d[nm]
                shape = list(ap.tensor.shape)
                o = nc.dram_tensor("DBGD_" + nm, shape, ap.dtype,
                                   kind="ExternalOutput").ap()
                mk.helpers["dma"]("sp", o, ap, [prog.buf("dbgd" + nm)],
                                  [prog.buf("dr:" + nm),
                                   prog.buf("dr:CIN0"), prog.buf("dr:COUT0"),
                                   prog.buf("dr:CIN1"), prog.buf("dr:COUT1"),
                                   prog.buf("dr:OCI0"), prog.buf("dr:OCO0"),
                                   prog.buf("dr:OCI1"), prog.buf("dr:OCO1"),
                                   prog.buf("dr:X1"), prog.buf("dr:BROWD")])
            for nm in dump_sbuf:
                t = mk.s[nm]
                shape = list(t[:].shape)
                o = nc.dram_tensor("DBG_" + nm, shape, t[:].dtype,
                                   kind="ExternalOutput").ap()
                deps = [bb for kk, bb in prog.bufs.items()
                        if kk == "sb:" + nm or kk.startswith(f"ar:{nm}:")]
                mk.helpers["dma"]("sp", o, t[:], [prog.buf("dbg" + nm)], deps)
            prog.finalize_and_emit()
    return nc



_NC_CACHE = {}


def kernel(**inp):
    key = (os.environ.get("MK_STOP"), os.environ.get("MK_DUMP_SBUF", ""),
           os.environ.get("MK_DUMP_DRAM", ""))
    if _NC_CACHE.get("key") != key:
        _NC_CACHE["prog"] = build_full(
            stop_after=key[0] or None,
            dump_sbuf=[x for x in key[1].split(",") if x])
        _NC_CACHE["key"] = key
    nc = _NC_CACHE["prog"]

    x = np.asarray(inp["L0_attn_hidden_states"])
    cosf, sinf = _rope_tables()
    w01t = np.ascontiguousarray(np.transpose(np.asarray(inp["L0_ffn_w1"]), (0, 2, 1)))
    w02t = np.ascontiguousarray(np.transpose(np.asarray(inp["L0_ffn_w2"]), (0, 2, 1)))
    g1t = np.ascontiguousarray(np.asarray(inp["L1_ffn_gate_w"]).T)
    w13t = np.ascontiguousarray(np.transpose(np.asarray(inp["L1_ffn_w13"]), (0, 2, 1)))
    w12t = np.ascontiguousarray(np.transpose(np.asarray(inp["L1_ffn_w2"]), (0, 2, 1)))
    sw13t = np.ascontiguousarray(np.asarray(inp["L1_ffn_sw13"]).T)
    sw2t = np.ascontiguousarray(np.asarray(inp["L1_ffn_sw2"]).T)
    relb = np.asarray(inp["L0_attn_rel_bias"])
    lam = np.asarray(inp["L1_attn_lam"])

    in_maps = []
    for c in range(8):
        s, j = c // 4, c % 4
        C = _host_constants(j)
        xs = np.ascontiguousarray(
            x[s, j * 1024:(j + 1) * 1024, :].T)
        cs = np.ascontiguousarray(cosf[j * 1024:(j + 1) * 1024].T)
        sn = np.ascontiguousarray(sinf[j * 1024:(j + 1) * 1024].T)
        cs2 = np.zeros((128, 1024), np.float32)
        cs2[0:64] = cs
        cs2[64:128] = cs
        sn2 = np.zeros((128, 1024), np.float32)
        sn2[0:64] = sn
        sn2[64:128] = sn
        m = dict(
            X0=xs,
            W0Q=np.asarray(inp["L0_attn_Wq"]), W0K=np.asarray(inp["L0_attn_Wk"]),
            W0V=np.asarray(inp["L0_attn_Wv"]), W0O=np.asarray(inp["L0_attn_Wo"]),
            W0G=np.asarray(inp["L0_ffn_wg"]), W01T=w01t, W02T=w02t,
            RELB=np.ascontiguousarray(relb[:, 8 * j:8 * j + 8]),
            W1Q=np.asarray(inp["L1_attn_Wq"]), W1K=np.asarray(inp["L1_attn_Wk"]),
            W1V=np.asarray(inp["L1_attn_Wv"]), W1O=np.asarray(inp["L1_attn_Wo"]),
            LAMC=np.ascontiguousarray(lam[4 * j:4 * j + 4]).reshape(1, 4),
            G1T=g1t, W13T=w13t, W12T=w12t, SW13T=sw13t, SW2T=sw2t,
            COS2=cs2.astype(BF), SIN2=sn2.astype(BF),
            MSK1D=C["MSK1"], CONSTS=C["CONSTS"],
            ONESBF=C["ONESBF"], ONESF=C["ONESF"],
            IDENF=C["IDENF"], IDENBF=C["IDENBF"],
            OH8=C["OH8"], OH16=C["OH16"],
            IDXALL=C["IDXALL"], OHB=C["OHB"],
        )
        in_maps.append(m)

    res = run_bass_kernel_spmd(nc, in_maps, core_ids=list(range(8)))
    _NC_CACHE["last_results"] = res.results
    out = np.empty((2, 4096, 2048), np.float32)
    for c in range(8):
        s, j = c // 4, c % 4
        out[s, j * 1024:(j + 1) * 1024, :] = res.results[c]["OUT"].T
    return out
